# version 13
# baseline (speedup 1.0000x reference)
import math
from contextlib import ExitStack

import numpy as np
import ml_dtypes

import concourse.bass as bass
import bass_rust
import concourse.mybir as mybir
from concourse.bass_utils import run_bass_kernel_spmd

F32 = mybir.dt.float32
BF16 = mybir.dt.bfloat16
I32 = mybir.dt.int32
AF = mybir.ActivationFunctionType
ALU = mybir.AluOpType
AX = mybir.AxisListType

f32 = np.float32
BF = ml_dtypes.bfloat16

NCORES = 8
B = 2
T = 4096
D = 4096
CHUNK = 512
TPC = 1024
NG = TPC // 128
CC = D // 128
RSQD = 1.0 / math.sqrt(128.0)
DEAD = -300.0

BUILD_PHASES = 8
DEBUG_TAPS = False



def _rope_tables():
    theta = (1.0 / (10000.0 ** (np.arange(0, 128, 2, dtype=np.float32) / np.float32(128.0)))).astype(np.float32)
    seq = np.arange(T, dtype=np.float32)
    idx = np.outer(seq, theta).astype(np.float32)
    idx2 = np.concatenate([idx, idx], axis=-1)
    return np.cos(idx2).astype(np.float32), np.sin(idx2).astype(np.float32)


def _host_prep(inp):
    x = np.asarray(inp["L0_attn_a_x"], np.float32)
    w_attn = np.asarray(inp["L0_attn_a_w_attn"], np.float32)
    w_proj = np.asarray(inp["L0_attn_a_w_proj"], np.float32)
    awte = np.asarray(inp["L0_attn_a_adapter_wte"], np.float32)
    gating = np.asarray(inp["L0_attn_a_gating"], np.float32)
    Wq = np.asarray(inp["L0_attn_b_Wq"], np.float32)
    bq = np.asarray(inp["L0_attn_b_bq"], np.float32)
    Wo = np.asarray(inp["L0_attn_b_Wo"], np.float32)
    M = np.asarray(inp["L0_attn_b_M"], np.float32)
    z = np.asarray(inp["L0_attn_b_z"], np.float32)
    qp = np.asarray(inp["L0_ffn_qweight_packed"], np.int32)
    fscale = np.asarray(inp["L0_ffn_scale"], np.float32).reshape(1, 1)
    fbias = np.asarray(inp["L0_ffn_bias"], np.float32).reshape(1, D)

    wattnT = np.ascontiguousarray(w_attn.T)
    wprojT = np.ascontiguousarray(w_proj.T)
    sel = (np.arange(8)[:, None] * 512 + np.arange(128)[None, :]).reshape(-1)
    wqselT = np.ascontiguousarray(Wq[sel].T)
    bqsel = np.ascontiguousarray(bq[sel].reshape(8, 128).T)
    woT = np.ascontiguousarray(Wo.T)
    awteT = np.ascontiguousarray(awte.T)
    qp2 = qp.reshape(D, D // 2)
    Mz = np.ascontiguousarray(np.concatenate([M, z[:, :, None]], axis=2).reshape(8 * 128, 129))
    cosT, sinT = _rope_tables()

    kj_idx = np.arange(128)[:, None]
    q_idx = np.arange(512)[None, :]
    bandq = np.concatenate(
        [(kj_idx + j * 128 <= q_idx).astype(np.float32) for j in range(4)], axis=1
    ).astype(BF)
    idn = np.eye(128, dtype=BF)

    in_maps = []
    for c in range(NCORES):
        p0, p1 = c, 7 - c
        own = np.ascontiguousarray(np.concatenate(
            [x[0, p0 * CHUNK:(p0 + 1) * CHUNK], x[1, p1 * CHUNK:(p1 + 1) * CHUNK]], axis=0))
        pos = np.concatenate([np.arange(p0 * CHUNK, (p0 + 1) * CHUNK),
                              np.arange(p1 * CHUNK, (p1 + 1) * CHUNK)])
        cost = np.ascontiguousarray(cosT[pos].T)
        sint = sinT[pos].T.copy()
        sint[:64] = -sint[:64]
        sint = np.ascontiguousarray(sint)
        dead = np.zeros((2, 8), np.float32)
        for kt in range(8):
            if kt >= p0:
                dead[0, kt] = DEAD
            if kt >= p1:
                dead[1, kt] = DEAD
        in_maps.append({
            "x_own": own,
            "wattnT": wattnT,
            "wprojT": wprojT,
            "wqselT": wqselT,
            "bqsel": bqsel,
            "woTs": np.ascontiguousarray(woT[4 * c * 128:(4 * c + 4) * 128]),
            "qps": np.ascontiguousarray(qp2[c * 512:(c + 1) * 512]),
            "awteT": awteT,
            "costab": cost,
            "sintabn": sint,
            "bandq": bandq,
            "deadoff": dead,
            "idn": idn,
            "Mz": Mz,
            "gating": np.ascontiguousarray(gating.reshape(1, 1)),
            "fscale": np.ascontiguousarray(fscale),
            "fbias": np.ascontiguousarray(fbias),
        })
    return in_maps



class Ctx:
    pass


def _mk(c, nm, instr, k=1):
    instr.then_inc(c.S[nm], k)
    c.N[nm] += k
    return c.N[nm]


def _ch(c, eng, nm, instr):
    _mk(c, nm, instr)
    eng.wait_ge(c.S[nm], c.N[nm])


def _mkcc(c, nm, instr):
    instr.then_inc(c.S[nm])
    c.N[nm] += 1
    return c.N[nm]


def _PT(c, i):
    bank = c.ps[7] if i == 0 else c.ps[5]
    return bank[:, 0:64].bitcast(BF16)


def _transpose_batch(c, pairs):
    block = c.block

    @block.tensor
    def _(te):
        base_v = c.N["sv"]
        te.wait_ge(c.S["sv"], base_v)
        te.wait_ge(c.S["sa"], c.N["sa"])
        for j, (src, dst) in enumerate(pairs):
            if j >= 2:
                te.wait_ge(c.S["sv"], base_v + j - 1)
            _mk(c, "sp", te.transpose(_PT(c, j % 2), src, c.idn[:, :]))

    @block.vector
    def _(v):
        base = c.N["sp"] - len(pairs)
        for j, (src, dst) in enumerate(pairs):
            v.wait_ge(c.S["sp"], base + j + 1)
            _mk(c, "sv", v.tensor_copy(dst, _PT(c, j % 2)))



def build_program():
    nc = bass.Bass("TRN2", target_bir_lowering=False)
    c = Ctx()
    c.nc = nc

    def din(name, shape, dt=F32):
        return nc.dram_tensor(name, shape, dt, kind="ExternalInput").ap()

    c.x_own = din("x_own", [TPC, D])
    c.wattnT = din("wattnT", [D, 3 * D])
    c.wprojT = din("wprojT", [D, D])
    c.wqselT = din("wqselT", [D, 1024])
    c.bqsel = din("bqsel", [128, 8])
    c.woTs = din("woTs", [512, D])
    c.qps = din("qps", [512, D // 2], I32)
    c.awteT = din("awteT", [D, 10])
    c.costab = din("costab", [128, TPC])
    c.sintabn = din("sintabn", [128, TPC])
    c.bandq = din("bandq", [128, 2048], BF16)
    c.deadoff = din("deadoff", [2, 8])
    c.idn_d = din("idn", [128, 128], BF16)
    c.Mz_d = din("Mz", [1024, 129])
    c.gating_d = din("gating", [1, 1])
    c.fscale_d = din("fscale", [1, 1])
    c.fbias_d = din("fbias", [1, D])

    c.out = nc.dram_tensor("out", [TPC, D], F32, kind="ExternalOutput").ap()

    def scr(name, shape, dt):
        return nc.dram_tensor(name, shape, dt, kind="Internal").ap()

    c.kc = scr("kc", [4128, TPC], BF16)
    c.kg = nc.dram_tensor("kg", [8 * 4128, TPC], BF16, kind="Internal", addr_space="Shared").ap()
    c.vc = scr("vc", [TPC, D], BF16)
    c.vg = nc.dram_tensor("vg", [8 * TPC, D], BF16, kind="Internal", addr_space="Shared").ap()
    c.wfc = scr("wfc", [D, 512], BF16)
    c.wfg = nc.dram_tensor("wfg", [8 * D, 512], BF16, kind="Internal", addr_space="Shared").ap()
    c.woc = scr("woc", [128, D], BF16)
    c.wog = nc.dram_tensor("wog", [1024, D], BF16, kind="Internal", addr_space="Shared").ap()
    c.qst = scr("qst", [D, TPC], BF16)
    c.x1 = scr("x1", [TPC, D], F32)
    c.x2 = scr("x2", [TPC, D], F32)
    c.x3 = scr("x3", [TPC, D], F32)

    c.taps = {}
    if DEBUG_TAPS:
        for nm, src, shape, dt in [
            ("tap_q", c.qst, [D, TPC], BF16),
            ("tap_kc", c.kc, [4128, TPC], BF16),
            ("tap_vc", c.vc, [TPC, D], BF16),
            ("tap_wf", c.wfg, [8 * D, 512], BF16),
            ("tap_wo", c.wog, [1024, D], BF16),
            ("tap_x1", c.x1, [TPC, D], F32),
            ("tap_x2", c.x2, [TPC, D], F32),
            ("tap_x3", c.x3, [TPC, D], F32),
        ]:
            c.taps[nm] = (nc.dram_tensor(nm, shape, dt, kind="ExternalOutput").ap(), src)

    with ExitStack() as st:
        en = st.enter_context

        def sb(name, shape, dt):
            return en(nc.sbuf_tensor(name, shape, dt))

        c.arena = sb("arena", [128, 32 * TPC], BF16)
        c.scr2 = sb("scr2", [128, 16384], BF16)
        c.cos_sb = sb("cos_sb", [128, TPC], F32)
        c.sin_sb = sb("sin_sb", [128, TPC], F32)
        c.band_sb = sb("band_sb", [128, 2048], BF16)
        c.idn = sb("idn_sb", [128, 128], BF16)
        c.dead_sb = sb("dead_sb", [128, 16], F32)
        c.Mz_sb = sb("Mz_sb", [128, 8 * 129], BF16)
        c.bq_sb = sb("bq_sb", [128, 8], F32)
        c.gat_sb = sb("gat_sb", [128, 1], F32)
        c.fsc_sb = sb("fsc_sb", [128, 1], F32)
        c.ak_sb = sb("ak_sb", [128, 320], BF16)
        c.av_sb = sb("av_sb", [10, D], BF16)
        c.awt_sb = sb("awt_sb", [128, 320], BF16)
        c.ones_bf = sb("ones_bf", [128, 1], BF16)
        c.ones_row = sb("ones_row", [1, 128], BF16)
        c.qn_sb = sb("qn_sb", [128, 256], F32)
        c.xt = [sb(f"xt{i}", [128, D], F32) for i in range(2)]
        c.htok = sb("htok", [128, D], BF16)
        c.wst = [sb(f"wst{i}", [128, 512], F32) for i in range(3)]
        c.wbf = [sb(f"wbf{i}", [128, 512], BF16) for i in range(4)]
        c.xs = [sb(f"xs{i}", [128, 512], F32) for i in range(2)]
        c.fb_sb = [sb(f"fb{i}", [128, 512], F32) for i in range(2)]
        c.ev = [sb(f"ev{i}", [128, 512], F32) for i in range(4)]
        c.bt = [sb(f"bt{i}", [128, 512], BF16) for i in range(4)]
        c.small = sb("small", [128, 64], F32)
        c.btab = sb("btab", [128, 10], F32)
        c.nmk = sb("nmk", [128, 2], F32)
        c.cmaxrow = sb("cmaxrow", [1, 64], F32)
        c.qmaxrow = sb("qmaxrow", [1, 64], F32)
        c.rrow = sb("rrow", [1, 512], F32)
        c.ones_row_f = sb("ones_row_f", [1, 128], F32)
        c.cmaxbf = sb("cmaxbf", [1, 64], BF16)
        c.tiny = sb("tiny", [1, 32], F32)
        c.tinybf = sb("tinybf", [1, 32], BF16)
        c.pa_sb = sb("pa_sb", [128, 10], BF16)
        c.paT_sb = sb("paT_sb", [10, 128], BF16)
        c.qp_sb = None
        c.dq_li = None

        c.ps = [en(nc.psum_tensor(f"ps{i}", [128, 512], F32)) for i in range(8)]

        c.S = {}
        c.N = {}
        for nm in ["sw", "sx", "sg", "sv", "sa", "sp", "pf",
                   "cfw", "cwo", "cka", "cva",
                   "ws0", "ws1", "ws2", "wb0", "wb1", "wb2", "wb3",
                   "kv0", "kv1", "lx0", "lx1", "xs0", "xs1", "fb0", "fb1", "qpl"]:
            c.S[nm] = en(nc.semaphore(nm))
            c.N[nm] = 0
        c.block = en(nc.Block())
        c.kvf = [0, 0]
        c.qf = [0, 0]
        c.xt_free_sp = 0
        c.ln_sg = 0
        c.evm = {}

        _emit(c)

    return nc


def _p5_views(c):
    xt0b = c.xt[0][:, :].bitcast(BF16)
    xt1b = c.xt[1][:, :].bitcast(BF16)
    k_sb = [c.scr2[:, 0:4608], xt0b[:, 0:4608]]
    v_sb = [c.scr2[:, 4608:9216], xt1b[:, 0:4608]]
    q_sb = [c.scr2[:, 11264:11776], xt0b[:, 4608:5120]]
    pt_ring = c.scr2[:, 9216:11264]
    rbc = c.scr2[:, 11776:12800].bitcast(F32)
    return k_sb, v_sb, pt_ring, q_sb, rbc


def _p7_views(c):
    return c.scr2[:, 0:8192], c.scr2[:, 8192:16384]



def _emit(c):
    _p0(c)
    if BUILD_PHASES >= 1:
        _p1_wf(c)
    if BUILD_PHASES >= 2:
        _p2_wo(c)
    if BUILD_PHASES >= 3:
        _p3_ln_hfm(c, c.x_own, first_ln=True)
        _p3_qk(c)
        _p3_v(c)
    if BUILD_PHASES >= 4:
        _p4_gather(c)
    if BUILD_PHASES >= 5:
        _p5_attn(c)
        _p5_proj(c)
    if BUILD_PHASES >= 6:
        _p3_ln_hfm(c, c.x1)
        _p6_ffn(c, c.x1, c.x2)
    if BUILD_PHASES >= 7:
        _p3_ln_hfm(c, c.x2)
        _p7_attnb(c)
    if BUILD_PHASES >= 8:
        _p3_ln_hfm(c, c.x3)
        _p6_ffn(c, c.x3, c.out)
    _final(c)


def _p0(c):
    block = c.block

    @block.sync
    def _(s):
        _mk(c, "sx", s.dma_start(out=c.cos_sb[:, :], in_=c.costab), 16)
        _mk(c, "sx", s.dma_start(out=c.sin_sb[:, :], in_=c.sintabn), 16)
        _mk(c, "sx", s.dma_start(out=c.band_sb[:, :], in_=c.bandq), 16)
        _mk(c, "sx", s.dma_start(out=c.idn[:, :], in_=c.idn_d), 16)
        _mk(c, "sx", s.dma_start(
            out=c.dead_sb[:, :],
            in_=c.deadoff.rearrange("a b -> (a b)").unsqueeze(0).partition_broadcast(128).squeeze(1)), 16)
        _mk(c, "sx", s.dma_start(out=c.bq_sb[:, :], in_=c.bqsel), 16)
        _mk(c, "sx", s.dma_start(
            out=c.gat_sb[:, :], in_=c.gating_d.partition_broadcast(128).squeeze(1)), 16)
        _mk(c, "sx", s.dma_start(
            out=c.fsc_sb[:, :], in_=c.fscale_d.partition_broadcast(128).squeeze(1)), 16)
        _mk(c, "sx", s.dma_start(
            out=c.xt[0][:, 0:1032],
            in_=c.Mz_d.rearrange("(a p) m -> p a m", p=128)), 16)
        _mk(c, "sx", s.dma_start(
            out=c.ev[0][:, 0:320],
            in_=c.awteT.rearrange("(cc p) a -> p cc a", p=128)), 16)
    sx_all = c.N["sx"]

    @block.gpsimd
    def _(g):
        g.memset(c.ones_bf[:, :], 1.0)
        g.memset(c.ones_row[:, :], 1.0)
        g.memset(c.ones_row_f[:, :], 1.0)
        _mk(c, "sg", g.engine_nop())

    @block.vector
    def _(v):
        v.wait_ge(c.S["sx"], sx_all)
        v.tensor_copy(c.Mz_sb[:, :], c.xt[0][:, 0:1032])
        _mk(c, "sv", v.tensor_copy(c.awt_sb[:, :], c.ev[0][:, 0:320]))


def _p1_wf(c):
    block = c.block
    dq = c.arena[:, 0:4096]
    c.qp_sb = c.scr2[:, 0:4096].bitcast(I32)
    c.dq_li = c.scr2[:, 4096:8192].bitcast(I32)

    for t in range(4):
        @block.sync
        def _(s, t=t):
            s.wait_ge(c.S["sv"], c.N["sv"])
            _mk(c, "qpl", s.dma_start(out=c.qp_sb[:, :], in_=c.qps[t * 128:(t + 1) * 128, :]), 16)

        @block.vector
        def _(v, t=t, qld=None):
            v.wait_ge(c.S["qpl"], c.N["qpl"])
            if t >= 1:
                v.wait_ge(c.S["sp"], c.N["sp"])
            _ch(c, v, "sv", v.tensor_scalar(c.dq_li[:, :], c.qp_sb[:, :], 15, None, ALU.bitwise_and))
            _ch(c, v, "sv", v.tensor_scalar(dq[:, 0:4096:2], c.dq_li[:, :], -8.0, None, ALU.add))
            _ch(c, v, "sv", v.tensor_scalar(c.dq_li[:, :], c.qp_sb[:, :], 4, 15,
                            ALU.logical_shift_right, ALU.bitwise_and))
            _mk(c, "sv", v.tensor_scalar(dq[:, 1:4096:2], c.dq_li[:, :], -8.0, None, ALU.add))

        @block.tensor
        def _(te, t=t):
            te.wait_ge(c.S["sv"], c.N["sv"])
            te.wait_ge(c.S["sa"], c.N["sa"])
            if t == 0:
                te.wait_ge(c.S["sx"], 16 * 4)

        base_v = c.N["sv"]
        for j in range(32):
            @block.tensor
            def _(te, t=t, j=j, base_v=base_v):
                if j >= 2:
                    te.wait_ge(c.S["sv"], base_v + j - 1)
                _mk(c, "sp", te.transpose(_PT(c, j % 2), dq[:, j * 128:(j + 1) * 128], c.idn[:, :]))

            @block.vector
            def _(v, t=t, j=j):
                v.wait_ge(c.S["sp"], c.N["sp"])
                if j >= 2:
                    v.wait_ge(c.S["sw"], c.N["sw"] - 16)
                _mk(c, "sv", v.tensor_copy(c.bt[2 + (j % 2)][:, 0:128], _PT(c, j % 2)))

            @block.sync
            def _(s, t=t, j=j):
                s.wait_ge(c.S["sv"], c.N["sv"])
                _mk(c, "sw", s.dma_start(
                    out=c.wfc[j * 128:(j + 1) * 128, t * 128:(t + 1) * 128],
                    in_=c.bt[2 + (j % 2)][:, 0:128]), 16)

    @block.gpsimd
    def _(g):
        g.wait_ge(c.S["sw"], c.N["sw"])
        _mkcc(c, "cfw", g.collective_compute(
            "AllGather", ALU.bypass, replica_groups=[list(range(NCORES))],
            ins=[c.wfc.opt()], outs=[c.wfg.opt()]))


def _p2_wo(c):
    block = c.block
    for sf in range(8):
        @block.sync
        def _(s, sf=sf):
            s.wait_ge(c.S["sv"], c.N["sv"])
            for j, (buf, sem) in enumerate(zip([c.wst[0], c.wst[1], c.ev[0], c.ev[1]],
                                               ["ws0", "ws1", "fb0", "fb1"])):
                _mk(c, sem, s.dma_start(
                    out=buf[:, :],
                    in_=c.woTs[j * 128:(j + 1) * 128, sf * 512:(sf + 1) * 512]), 16)
        p2l = [c.N[s2] for s2 in ("ws0", "ws1", "fb0", "fb1")]

        @block.vector
        def _(v, sf=sf, p2l=p2l):
            for s2, val in zip(("ws0", "ws1", "fb0", "fb1"), p2l):
                v.wait_ge(c.S[s2], val)
            v.tensor_add(c.ev[2][:, :], c.wst[0][:, :], c.wst[1][:, :])
            _ch(c, v, "sv", v.tensor_add(c.ev[3][:, :], c.ev[0][:, :], c.ev[1][:, :]))
            _mk(c, "sv", v.tensor_add(c.bt[0][:, :], c.ev[2][:, :], c.ev[3][:, :]))

        @block.sync
        def _(s, sf=sf):
            s.wait_ge(c.S["sv"], c.N["sv"])
            _mk(c, "sw", s.dma_start(
                out=c.woc[:, sf * 512:(sf + 1) * 512], in_=c.bt[0][:, :]), 16)

    @block.gpsimd
    def _(g):
        g.wait_ge(c.S["sw"], c.N["sw"])
        _mkcc(c, "cwo", g.collective_compute(
            "AllGather", ALU.bypass, replica_groups=[list(range(NCORES))],
            ins=[c.woc.opt()], outs=[c.wog.opt()]))


def _p3_ln_hfm(c, xsrc, first_ln=False):
    block = c.block
    sgbase = c.N["sg"]
    c.htok_free_sp = c.N["sp"]

    for g in range(NG):
        xb = c.xt[g % 2]
        col = (g % 2) * 16

        @block.sync
        def _(s, g=g, xb=xb):
            s.wait_ge(c.S["sg"], c.N["sg"])
            if g < 2:
                s.wait_ge(c.S["sp"], c.xt_free_sp)
            if g == 0:
                s.wait_ge(c.S["sw"], c.N["sw"])
                s.wait_ge(c.S["pf"], c.N["pf"])
                if first_ln:
                    s.wait_ge(c.S["sv"], c.N["sv"])
            _mk(c, f"lx{g % 2}", s.dma_start(out=xb[:, :], in_=xsrc[g * 128:(g + 1) * 128, :]), 16)
        xld = c.N[f"lx{g % 2}"]

        @block.vector
        def _(v, g=g, xb=xb, col=col, sgbase=sgbase, xld=xld):
            v.wait_ge(c.S[f"lx{g % 2}"], xld)
            if g >= 2:
                v.wait_ge(c.S["sg"], sgbase + g - 1)
            _mk(c, "sv", v.reduce_sum(c.small[:, col + 0:col + 1], xb[:, :], axis=AX.X))

        @block.scalar
        def _(sc, g=g, xb=xb, col=col, xld=xld):
            sc.wait_ge(c.S[f"lx{g % 2}"], xld)
            sc.wait_ge(c.S["sp"], c.htok_free_sp)
            _mk(c, "sa", sc.activation(c.htok[:, :], xb[:, :], AF.Square,
                                       accum_out=c.small[:, col + 1:col + 2]))

        @block.vector
        def _(v, g=g, col=col):
            v.wait_ge(c.S["sa"], c.N["sa"])
            _ch(c, v, "sv", v.tensor_scalar(c.small[:, col + 2:col + 3], c.small[:, col + 0:col + 1],
                            1.0 / D, None, ALU.mult))
            _ch(c, v, "sv", v.tensor_mul(c.small[:, col + 3:col + 4], c.small[:, col + 2:col + 3],
                         c.small[:, col + 2:col + 3]))
            _mk(c, "sv", v.scalar_tensor_tensor(
                c.small[:, col + 4:col + 5], c.small[:, col + 1:col + 2], 1.0 / D,
                c.small[:, col + 3:col + 4], ALU.mult, ALU.subtract))

        @block.scalar
        def _(sc, g=g, col=col):
            sc.wait_ge(c.S["sv"], c.N["sv"])
            _ch(c, sc, "sa", sc.activation(c.small[:, col + 5:col + 6], c.small[:, col + 4:col + 5],
                          AF.Ln, bias=f32(1e-5), scale=f32(1.0)))
            _mk(c, "sa", sc.activation(c.small[:, col + 6:col + 7], c.small[:, col + 5:col + 6],
                                       AF.Exp, bias=f32(0.0), scale=f32(-0.5)))

        @block.vector
        def _(v, g=g, col=col):
            v.wait_ge(c.S["sa"], c.N["sa"])
            _mk(c, "sv", v.scalar_tensor_tensor(
                c.small[:, col + 7:col + 8], c.small[:, col + 2:col + 3], -1.0,
                c.small[:, col + 6:col + 7], ALU.mult, ALU.mult))

        @block.gpsimd
        def _(gp, g=g, xb=xb, col=col):
            gp.wait_ge(c.S["sv"], c.N["sv"])
            gp.wait_ge(c.S["sp"], c.htok_free_sp)
            _mk(c, "sg", gp.tensor_scalar(
                c.htok[:, :], xb[:, :], c.small[:, col + 6:col + 7],
                c.small[:, col + 7:col + 8], ALU.mult, ALU.add))

        @block.tensor
        def _(te, g=g):
            te.wait_ge(c.S["sg"], c.N["sg"])
        _transpose_batch(c, [
            (c.htok[:, j * 128:(j + 1) * 128],
             c.arena[:, j * TPC + g * 128: j * TPC + (g + 1) * 128])
            for j in range(CC)
        ])
        c.htok_free_sp = c.N["sp"]
    c.ln_sg = c.N["sg"]


def _p3_qk(c):
    block = c.block
    c.wsm, c.wdm, c.wpm = {}, {}, {}
    c.wsm_e = ("sv", c.N["sv"])
    c.wpm_e = c.N["sp"]

    for ot in range(64):
        is_q = ot < 32
        hh = ot if is_q else ot - 32

        for cq in range(8):
            i = ot * 8 + cq

            @block.sync
            def _(s, ot=ot, cq=cq, i=i):
                w = c.wsm.get(i - 3, c.wsm_e)
                s.wait_ge(c.S[w[0]], w[1])
                src = c.wattnT.rearrange("(cq ch p) (ot j) -> cq ot p ch j",
                                         ch=4, p=128, j=128)[cq, ot]
                ins = s.dma_start(out=c.wst[i % 3][:, :], in_=src)
                c.wdm[i] = (f"ws{i % 3}", _mk(c, f"ws{i % 3}", ins, 16))

            @block.vector
            def _(v, cq=cq, i=i):
                v.wait_ge(c.S[c.wdm[i][0]], c.wdm[i][1])
                v.wait_ge(c.S["sp"], c.wpm.get(i - 4, c.wpm_e))
                c.wsm[i] = ("sv", _mk(c, "sv", v.tensor_copy(c.wbf[i % 4][:, :], c.wst[i % 3][:, :])))

            @block.tensor
            def _(te, ot=ot, cq=cq, i=i, is_q=is_q):
                te.wait_ge(c.S[c.wsm[i][0]], c.wsm[i][1])
                if cq == 0:
                    te.wait_ge(c.S["sa"], c.N["sa"])
                last = None
                for ch in range(4):
                    cc = cq * 4 + ch
                    lhs = c.wbf[i % 4][:, ch * 128:(ch + 1) * 128]
                    for th in range(2):
                        last = te.matmul(
                            c.ps[th][:, :], lhs,
                            c.arena[:, cc * TPC + th * 512: cc * TPC + th * 512 + 512],
                            start=(cc == 0), stop=(cc == CC - 1))
                    if not is_q:
                        last = te.matmul(
                            c.ps[2][:, 0:10], lhs, c.awt_sb[:, cc * 10:(cc + 1) * 10],
                            start=(cc == 0), stop=(cc == CC - 1))
                c.wpm[i] = _mk(c, "sp", last)

        mm_done = c.N["sp"]
        for th in range(2):
            @block.vector
            def _(v, ot=ot, th=th, mm_done=mm_done):
                v.wait_ge(c.S["sp"], mm_done)
                if th == 0:
                    v.wait_ge(c.S["sw"], c.N["sw"])
                ps = c.ps[th]
                cs = c.cos_sb[:, th * 512:(th + 1) * 512]
                sn = c.sin_sb[:, th * 512:(th + 1) * 512]
                v.tensor_mul(c.ev[0][:, :], ps[:, :], cs)
                v.tensor_mul(c.ev[1][0:64, :], ps[64:128, :], sn[0:64, :])
                _ch(c, v, "sv", v.tensor_mul(c.ev[1][64:128, :], ps[0:64, :], sn[64:128, :]))
                _ch(c, v, "sv", v.tensor_add(c.bt[th][:, :], c.ev[0][:, :], c.ev[1][:, :]))
                _mk(c, "sv", v.tensor_mul(c.bt[2 + th][:, :], c.bt[th][:, :], c.bt[th][:, :]))

            @block.tensor
            def _(te, th=th):
                te.wait_ge(c.S["sv"], c.N["sv"])
                _mk(c, "sp", te.matmul(c.ps[3][0:1, :], c.ones_bf[:, :],
                                       c.bt[2 + th][:, :], start=True, stop=True))

            @block.vector
            def _(v, th=th, hh=hh, is_q=is_q):
                v.wait_ge(c.S["sp"], c.N["sp"])
                dst = (c.qmaxrow if is_q else c.cmaxrow)[0:1, hh * 2 + th: hh * 2 + th + 1]
                _mk(c, "sv", v.reduce_max(dst, c.ps[3][0:1, :], axis=AX.X))

            @block.sync
            def _(s, ot=ot, th=th, hh=hh, is_q=is_q):
                s.wait_ge(c.S["sv"], c.N["sv"])
                if is_q:
                    _mk(c, "sw", s.dma_start(
                        out=c.qst[ot * 128:(ot + 1) * 128, th * 512:(th + 1) * 512],
                        in_=c.bt[th][:, :]), 16)
                else:
                    _mk(c, "sw", s.dma_start(
                        out=c.kc[hh * 128:(hh + 1) * 128, th * 512:(th + 1) * 512],
                        in_=c.bt[th][:, :]), 16)

        if not is_q:
            @block.scalar
            def _(sc, hh=hh):
                sc.wait_ge(c.S["sp"], c.N["sp"])
                _mk(c, "sa", sc.copy(c.ak_sb[:, hh * 10:(hh + 1) * 10], c.ps[2][:, 0:10]))

    c.qn_ready = c.N["sa"]

    @block.vector
    def _(v):
        _mk(c, "sv", v.tensor_copy(c.cmaxbf[0:1, :], c.cmaxrow[0:1, :]))

    @block.sync
    def _(s):
        s.wait_ge(c.S["sv"], c.N["sv"])
        _mk(c, "sw", s.dma_start(
            out=c.kc[4096:4128, 0:2],
            in_=c.cmaxbf[0:1, :]), 16)


def _p3_v(c):
    block = c.block

    def vsrc(cc, ov):
        return c.wattnT.rearrange("(cc p) o -> cc p o", p=128)[cc][
            :, 2 * D + ov * 512: 2 * D + (ov + 1) * 512]

    c.wsm, c.wdm, c.wpm = {}, {}, {}
    c.wsm_e = ("sg", c.N["sg"])
    c.wpm_e = c.N["sp"]
    for ov in range(8):
        for cc in range(CC):
            i = ov * 64 + cc

            @block.sync
            def _(s, ov=ov, cc=cc, i=i):
                w = c.wsm.get(i - 3, c.wsm_e)
                s.wait_ge(c.S[w[0]], w[1])
                ins = s.dma_start(out=c.wst[i % 3][:, :], in_=vsrc(cc, ov))
                c.wdm[i] = (f"ws{i % 3}", _mk(c, f"ws{i % 3}", ins, 16))

            @block.vector
            def _(v, cc=cc, i=i):
                v.wait_ge(c.S[c.wdm[i][0]], c.wdm[i][1])
                v.wait_ge(c.S["sp"], c.wpm.get(i - 4, c.wpm_e))
                c.wsm[i] = ("sv", _mk(c, "sv", v.tensor_copy(c.wbf[i % 4][:, :], c.wst[i % 3][:, :])))

            @block.tensor
            def _(te, ov=ov, cc=cc, i=i):
                te.wait_ge(c.S[c.wsm[i][0]], c.wsm[i][1])
                last = None
                for g in range(NG):
                    last = te.matmul(
                        c.ps[g][:, :],
                        c.arena[:, cc * TPC + g * 128: cc * TPC + (g + 1) * 128],
                        c.wbf[i % 4][:, :], start=(cc == 0), stop=(cc == CC - 1))
                c.wpm[i] = _mk(c, "sp", last)

        mm_done = c.N["sp"]
        for g in range(NG):
            @block.vector
            def _(v, g=g, mm_done=mm_done):
                v.wait_ge(c.S["sp"], mm_done)
                v.wait_ge(c.S["sw"], c.N["sw"])
                _mk(c, "sv", v.tensor_copy(c.bt[g % 2][:, :], c.ps[g][:, :]))

            @block.sync
            def _(s, ov=ov, g=g):
                s.wait_ge(c.S["sv"], c.N["sv"])
                _mk(c, "sw", s.dma_start(
                    out=c.vc[g * 128:(g + 1) * 128, ov * 512:(ov + 1) * 512],
                    in_=c.bt[g % 2][:, :]), 16)

        for cc in range(CC):
            i = ov * 64 + 32 + cc

            @block.sync
            def _(s, ov=ov, cc=cc, i=i):
                w = c.wsm.get(i - 3, c.wsm_e)
                s.wait_ge(c.S[w[0]], w[1])
                ins = s.dma_start(out=c.wst[i % 3][:, :], in_=vsrc(cc, ov))
                c.wdm[i] = (f"ws{i % 3}", _mk(c, f"ws{i % 3}", ins, 16))

            @block.vector
            def _(v, cc=cc, i=i):
                v.wait_ge(c.S[c.wdm[i][0]], c.wdm[i][1])
                v.wait_ge(c.S["sp"], c.wpm.get(i - 4, c.wpm_e))
                c.wsm[i] = ("sv", _mk(c, "sv", v.tensor_copy(c.wbf[i % 4][:, :], c.wst[i % 3][:, :])))

            @block.tensor
            def _(te, cc=cc, i=i):
                te.wait_ge(c.S[c.wsm[i][0]], c.wsm[i][1])
                ins = te.matmul(c.ps[0][0:10, :], c.awt_sb[:, cc * 10:(cc + 1) * 10],
                                c.wbf[i % 4][:, :], start=(cc == 0), stop=(cc == CC - 1))
                c.wpm[i] = _mk(c, "sp", ins)

        @block.vector
        def _(v, ov=ov):
            v.wait_ge(c.S["sp"], c.N["sp"])
            _mk(c, "sv", v.tensor_copy(c.av_sb[0:10, ov * 512:(ov + 1) * 512],
                                       c.ps[0][0:10, :]))


def _p4_gather(c):
    block = c.block

    @block.gpsimd
    def _(g):
        g.wait_ge(c.S["sw"], c.N["sw"])
        _mkcc(c, "cka", g.collective_compute(
            "AllGather", ALU.bypass, replica_groups=[list(range(NCORES))],
            ins=[c.kc.opt()], outs=[c.kg.opt()]))
        _mkcc(c, "cva", g.collective_compute(
            "AllGather", ALU.bypass, replica_groups=[list(range(NCORES))],
            ins=[c.vc.opt()], outs=[c.vg.opt()]))


def _p5_attn(c):
    block = c.block
    k_sbs, v_sbs, pt_ring, q_sbs, rbc = _p5_views(c)

    first = True
    prev_evac_sv = c.N["sv"]
    den_mark = {}
    pa_exp_mark = c.qn_ready
    c.bcast_done = 0
    for b in range(B):
        for h in range(32):
            par = (b * 32 + h) % 2
            k_sb = k_sbs[par]
            v_sb = v_sbs[par]
            q_sb = q_sbs[par]
            kv_free = c.kvf[par]
            q_free = c.qf[par]
            ring_guard = [den_mark.get(32 + j, 0) for j in range(4)]

            @block.sync
            def _(s, b=b, h=h, first=first, kv_free=kv_free, q_free=q_free,
                  k_sb=k_sb, q_sb=q_sb, par=par):
                if first:
                    s.wait_ge(c.S["cka"], 1)
                if par == 1:
                    s.wait_ge(c.S["sg"], c.ln_sg)
                s.wait_ge(c.S["sp"], kv_free)
                kv = f"kv{par}"
                if b == 0:
                    kgv = c.kg.rearrange("(r q) t -> q r t", q=4128)
                    _mk(c, kv, s.dma_start(
                        out=k_sb[:, 0:4096],
                        in_=kgv[h * 128:(h + 1) * 128, :, 0:512]), 16)
                else:
                    kap = bass_rust.AP(c.kg.tensor, 7 * 4128 * 1024 + h * 128 * 1024 + 512,
                                       [[1024, 128], [-4128 * 1024, 8], [1, 512]])
                    _mk(c, kv, s.dma_start(out=k_sb[:, 0:4096], in_=kap), 16)
                _mk(c, kv, s.dma_start(
                    out=k_sb[:, 8 * 512:9 * 512],
                    in_=c.kc[h * 128:(h + 1) * 128, b * 512:(b + 1) * 512]), 16)
                s.wait_ge(c.S["sp"], q_free)
                _mk(c, kv, s.dma_start(
                    out=q_sb[:, :],
                    in_=c.qst[h * 128:(h + 1) * 128, b * 512:(b + 1) * 512]), 16)
            kvsem = f"kv{par}"

            @block.gpsimd
            def _(g, b=b, h=h, first=first, kv_free=kv_free, v_sb=v_sb, par=par):
                if first:
                    g.wait_ge(c.S["cva"], 1)
                    g.wait_ge(c.S["cka"], 1)
                if par == 1:
                    g.wait_ge(c.S["sg"], c.ln_sg)
                g.wait_ge(c.S["sp"], kv_free)
                kv = f"kv{par}"
                for kt in range(8):
                    r = kt if b == 0 else 7 - kt
                    vsrc = c.vg[r * TPC + b * 512: r * TPC + b * 512 + 512,
                                h * 128:(h + 1) * 128].rearrange("(j p) e -> p j e", p=128)
                    _mk(c, kv, g.dma_start(
                        out=v_sb[:, kt * 512:(kt + 1) * 512], in_=vsrc), 16)
                _mk(c, kv, g.dma_start(
                    out=v_sb[:, 4096:4608],
                    in_=c.vc[b * 512:(b + 1) * 512,
                             h * 128:(h + 1) * 128].rearrange("(j p) e -> p j e", p=128)), 16)
                _mk(c, kv, g.dma_start(
                    out=c.tinybf[0:1, 0:16],
                    in_=c.kg.rearrange("(r q) t -> q r t", q=4128)[4096 + h][:, 0:2]), 16)
            kvload = c.N[kvsem]

            @block.vector
            def _(v, b=b, h=h, kvsem=kvsem, kvload=kvload, bc=c.bcast_done):
                v.wait_ge(c.S[kvsem], kvload)
                v.wait_ge(c.S["sp"], bc)
                _ch(c, v, "sv", v.reduce_max(c.tiny[0:1, 0:1], c.tinybf[0:1, 0:16], axis=AX.X))
                _mk(c, "sv", v.tensor_mul(c.tiny[0:1, 3:4], c.tiny[0:1, 0:1],
                                          c.qmaxrow[0:1, h * 2 + b: h * 2 + b + 1]))

            @block.scalar
            def _(sc):
                sc.wait_ge(c.S["sv"], c.N["sv"])
                _ch(c, sc, "sa", sc.activation(c.tiny[0:1, 4:5], c.tiny[0:1, 3:4], AF.Ln,
                                               bias=f32(0.0), scale=f32(1.0)))
                _mk(c, "sa", sc.activation(c.tiny[0:1, 5:6], c.tiny[0:1, 4:5], AF.Exp,
                                           bias=f32(0.0), scale=f32(0.5)))

            @block.vector
            def _(v):
                v.wait_ge(c.S["sa"], c.N["sa"])
                _ch(c, v, "sv", v.tensor_scalar(c.tiny[0:1, 6:7], c.tiny[0:1, 5:6],
                                                -RSQD, None, ALU.mult))
                _mk(c, "sv", v.tensor_copy(c.tinybf[0:1, 16:17], c.tiny[0:1, 6:7]))

            @block.tensor
            def _(te):
                te.wait_ge(c.S["sv"], c.N["sv"])
                _mk(c, "sp", te.matmul(c.ps[7][:, 256:257], c.ones_row[0:1, :],
                                       c.tinybf[0:1, 16:17], start=True, stop=True))
            c.bcast_done = c.N["sp"]

            @block.vector
            def _(v, b=b):
                v.wait_ge(c.S["sp"], c.N["sp"])
                v.wait_ge(c.S["sa"], c.N["sa"])
                _ch(c, v, "sv", v.tensor_copy(c.nmk[:, 0:1], c.ps[7][:, 256:257]))
                _ch(c, v, "sv", v.tensor_scalar(c.btab[:, 0:8], c.dead_sb[:, b * 8:(b + 1) * 8],
                                                c.nmk[:, 0:1], None, ALU.add))
                _mk(c, "sv", v.tensor_copy(c.btab[:, 8:9], c.nmk[:, 0:1]))
            bias_ready = c.N["sv"]

            score_mark = {}
            sa0 = c.N["sa"]
            sg0 = c.N["sg"]

            @block.tensor
            def _(te, kvsem=kvsem, kvload=kvload, prev_evac_sv=prev_evac_sv, sa0=sa0, sg0=sg0,
                  k_sb=k_sb, v_sb=v_sb, q_sb=q_sb):
                sbank = [0, 1, 5, 6]
                for step in range(40):
                    ss = step
                    if ss < 36:
                        if ss == 0:
                            te.wait_ge(c.S[kvsem], kvload)
                            te.wait_ge(c.S["sv"], prev_evac_sv)
                        if ss < 4:
                            te.wait_ge(c.S["sa"], sa0)
                        else:
                            te.wait_ge(c.S["sa"], sa0 + ss - 3)
                        ins = te.matmul(c.ps[sbank[ss % 4]][:, :],
                                        k_sb[:, ss * 128:(ss + 1) * 128],
                                        q_sb[:, :], start=True, stop=True)
                        score_mark[ss] = _mk(c, "sp", ins)
                    if step >= 4:
                        sy = step - 4
                        te.wait_ge(c.S["sa"], sa0 + sy + 1)
                        if sy >= 32:
                            te.wait_ge(c.S["sg"], sg0 + (sy - 32) + 1)
                        te.matmul(c.ps[2][:, :], v_sb[:, sy * 128:(sy + 1) * 128],
                                  pt_ring[:, (sy % 4) * 512:(sy % 4 + 1) * 512],
                                  start=(sy == 0), stop=(sy == 35))
                        ins = te.matmul(c.ps[4][0:1, :], c.ones_bf[:, :],
                                        pt_ring[:, (sy % 4) * 512:(sy % 4 + 1) * 512],
                                        start=(sy == 0), stop=(sy == 35))
                        den_mark[sy] = _mk(c, "sp", ins)

            @block.scalar
            def _(sc, bias_ready=bias_ready, ring_guard=ring_guard):
                sc.wait_ge(c.S["sv"], bias_ready)
                for ss in range(36):
                    slot = ss // 4
                    col = slot if slot < 8 else 8
                    sc.wait_ge(c.S["sp"], score_mark[ss])
                    if ss >= 4:
                        sc.wait_ge(c.S["sp"], den_mark[ss - 4])
                    elif ring_guard[ss] > 0:
                        sc.wait_ge(c.S["sp"], ring_guard[ss])
                    _mk(c, "sa", sc.activation(
                        pt_ring[:, (ss % 4) * 512:(ss % 4 + 1) * 512],
                        c.ps[[0, 1, 5, 6][ss % 4]][:, :], AF.Exp,
                        bias=c.btab[:, col:col + 1], scale=f32(RSQD)))

            @block.gpsimd
            def _(g, sa0=sa0):
                for ss in range(32, 36):
                    g.wait_ge(c.S["sa"], sa0 + ss + 1)
                    _mk(c, "sg", g.tensor_mul(
                        pt_ring[:, (ss % 4) * 512:(ss % 4 + 1) * 512],
                        pt_ring[:, (ss % 4) * 512:(ss % 4 + 1) * 512],
                        c.band_sb[:, (ss - 32) * 512:(ss - 31) * 512]))

            for qt in range(4):
                @block.tensor
                def _(te, h=h, qt=qt, kvsem=kvsem, kvload=kvload, pa_exp_mark=pa_exp_mark, q_sb=q_sb):
                    te.wait_ge(c.S[kvsem], kvload)
                    te.wait_ge(c.S["sa"], pa_exp_mark)
                    te.wait_ge(c.S["sv"], c.N["sv"])
                    _mk(c, "sp", te.matmul(c.ps[7][:, 0:10], q_sb[:, qt * 128:(qt + 1) * 128],
                                           c.ak_sb[:, h * 10:(h + 1) * 10],
                                           start=True, stop=True))

                @block.vector
                def _(v):
                    v.wait_ge(c.S["sp"], c.N["sp"])
                    _ch(c, v, "sv", v.reduce_max(c.small[:, 32:33], c.ps[7][:, 0:10], axis=AX.X))
                    _mk(c, "sv", v.tensor_scalar(c.small[:, 33:34], c.small[:, 32:33],
                                                 -1.0 / 64.0, None, ALU.mult))

                @block.scalar
                def _(sc):
                    sc.wait_ge(c.S["sv"], c.N["sv"])
                    _mk(c, "sa", sc.activation(c.pa_sb[:, :], c.ps[7][:, 0:10], AF.Exp,
                                               bias=c.small[:, 33:34], scale=f32(1.0 / 64.0),
                                               accum_out=c.small[:, 34:35]))
                pa_exp_mark = c.N["sa"]

                @block.vector
                def _(v):
                    v.wait_ge(c.S["sa"], c.N["sa"])
                    _ch(c, v, "sv", v.reciprocal(c.small[:, 35:36], c.small[:, 34:35]))
                    _mk(c, "sv", v.tensor_scalar(c.small[:, 36:37], c.small[:, 35:36],
                                                 c.gat_sb[:, 0:1], None, ALU.mult))

                @block.gpsimd
                def _(g):
                    g.wait_ge(c.S["sv"], c.N["sv"])
                    _mk(c, "sg", g.tensor_scalar(c.pa_sb[:, :], c.pa_sb[:, :],
                                                 c.small[:, 36:37], None, ALU.mult))

                @block.tensor
                def _(te):
                    te.wait_ge(c.S["sg"], c.N["sg"])
                    te.wait_ge(c.S["sv"], c.N["sv"])
                    _mk(c, "sp", te.transpose(
                        c.ps[7][:, 64:128].bitcast(BF16)[0:10, :], c.pa_sb[:, :], c.idn[:, :]))

                @block.vector
                def _(v):
                    v.wait_ge(c.S["sp"], c.N["sp"])
                    _mk(c, "sv", v.tensor_copy(c.paT_sb[0:10, :],
                                               c.ps[7][:, 64:128].bitcast(BF16)[0:10, :]))

                @block.tensor
                def _(te, h=h, qt=qt):
                    te.wait_ge(c.S["sv"], c.N["sv"])
                    _mk(c, "sp", te.matmul(c.ps[3][:, qt * 128:(qt + 1) * 128],
                                           c.av_sb[0:10, h * 128:(h + 1) * 128],
                                           c.paT_sb[0:10, :], start=True, stop=True))
            c.qf[par] = c.N["sp"]
            c.kvf[par] = den_mark[35]

            @block.vector
            def _(v):
                v.wait_ge(c.S["sp"], c.N["sp"])
                _mk(c, "sv", v.reciprocal(c.rrow[0:1, :], c.ps[4][0:1, 0:512]))

            @block.tensor
            def _(te):
                te.wait_ge(c.S["sv"], c.N["sv"])
                te.wait_ge(c.S["sa"], c.N["sa"])
                _mk(c, "sp", te.matmul(c.ps[7][:, 0:512], c.ones_row_f[0:1, :],
                                       c.rrow[0:1, :], start=True, stop=True))

            @block.vector
            def _(v, b=b, h=h):
                v.wait_ge(c.S["sp"], c.N["sp"])
                _ch(c, v, "sv", v.tensor_copy(rbc[:, :], c.ps[7][:, 0:512]))
                _ch(c, v, "sv", v.tensor_mul(c.ev[0][:, :], c.ps[2][:, :], rbc[:, :]))
                _mk(c, "sv", v.tensor_add(
                    c.arena[:, h * TPC + b * 512: h * TPC + (b + 1) * 512],
                    c.ev[0][:, :], c.ps[3][:, :]))
            prev_evac_sv = c.N["sv"]

            first = False
    c.xt_free_sp = c.N["sp"]


def _p5_proj(c):
    block = c.block

    c.wsm, c.wdm, c.wpm = {}, {}, {}
    c.wsm_e = ("sg", c.N["sg"])
    c.wpm_e = c.N["sp"]
    for po in range(8):
        for cc in range(CC):
            i = po * 32 + cc

            @block.sync
            def _(s, po=po, cc=cc, i=i):
                w = c.wsm.get(i - 3, c.wsm_e)
                s.wait_ge(c.S[w[0]], w[1])
                src = c.wprojT.rearrange("(cc p) o -> cc p o", p=128)[cc][
                    :, po * 512:(po + 1) * 512]
                ins = s.dma_start(out=c.wst[i % 3][:, :], in_=src)
                c.wdm[i] = (f"ws{i % 3}", _mk(c, f"ws{i % 3}", ins, 16))

            @block.vector
            def _(v, cc=cc, i=i):
                v.wait_ge(c.S[c.wdm[i][0]], c.wdm[i][1])
                v.wait_ge(c.S["sp"], c.wpm.get(i - 4, c.wpm_e))
                c.wsm[i] = ("sv", _mk(c, "sv", v.tensor_copy(c.wbf[i % 4][:, :], c.wst[i % 3][:, :])))

            @block.tensor
            def _(te, po=po, cc=cc, i=i):
                te.wait_ge(c.S[c.wsm[i][0]], c.wsm[i][1])
                if cc == 0:
                    if po == 0:
                        te.wait_ge(c.S["sa"], c.N["sa"])
                    for g in range(8):
                        te.wait_ge(c.S["sv"], c.evm.get(g, 0))
                last = None
                for g in range(8):
                    last = te.matmul(
                        c.ps[g][:, :],
                        c.arena[:, cc * TPC + g * 128: cc * TPC + (g + 1) * 128],
                        c.wbf[i % 4][:, :], start=(cc == 0), stop=(cc == CC - 1))
                c.wpm[i] = _mk(c, "sp", last)

        _evac_residual(c, po, c.x_own, c.x1, scale_bias=False, final_out=False)


def _evac_residual(c, og, xin, xout, scale_bias, final_out):
    block = c.block
    mm_done = c.N["sp"]

    fbl = 0
    if scale_bias:
        @block.scalar
        def _(s, og=og):
            s.wait_ge(c.S["sv"], c.N["sv"])
            _mk(c, f"fb{og % 2}", s.dma_start(
                out=c.fb_sb[og % 2][:, :],
                in_=c.fbias_d[0:1, og * 512:(og + 1) * 512].partition_broadcast(128).squeeze(1)), 16)
        fbl = c.N[f"fb{og % 2}"]

    for g in range(8):
        @block.scalar
        def _(s, og=og, g=g):
            s.wait_ge(c.S["pf"], max(0, c.N["pf"] - 16))
            _mk(c, f"xs{g % 2}", s.dma_start(
                out=c.xs[g % 2][:, :],
                in_=xin[g * 128:(g + 1) * 128, og * 512:(og + 1) * 512]), 16)
        xsl = c.N[f"xs{g % 2}"]

        @block.vector
        def _(v, og=og, g=g, mm_done=mm_done, xsl=xsl, fbl=fbl):
            v.wait_ge(c.S["sp"], mm_done)
            v.wait_ge(c.S[f"xs{g % 2}"], xsl)
            if scale_bias:
                v.wait_ge(c.S[f"fb{og % 2}"], fbl)
            v.wait_ge(c.S["pf"], max(0, c.N["pf"] - 16))
            if scale_bias:
                _ch(c, v, "sv", v.scalar_tensor_tensor(c.ev[2 + g % 2][:, :], c.ps[g][:, :],
                                       c.fsc_sb[:, 0:1], c.fb_sb[og % 2][:, :],
                                       ALU.mult, ALU.add))
                c.evm[g] = _mk(c, "sv", v.tensor_add(c.ev[g % 2][:, :], c.ev[2 + g % 2][:, :],
                                          c.xs[g % 2][:, :]))
            else:
                c.evm[g] = _mk(c, "sv", v.tensor_add(c.ev[g % 2][:, :], c.ps[g][:, :],
                                                     c.xs[g % 2][:, :]))

        @block.scalar
        def _(s, og=og, g=g):
            s.wait_ge(c.S["sv"], c.N["sv"])
            _mk(c, "pf", s.dma_start(
                out=xout[g * 128:(g + 1) * 128, og * 512:(og + 1) * 512],
                in_=c.ev[g % 2][:, :]), 16)


def _p6_ffn(c, xin, xout):
    block = c.block
    final_out = xout is c.out

    c.wdm, c.wpm = {}, {}
    c.wpm_e = c.N["sp"]
    for og in range(8):
        for cc in range(CC):
            i = og * 32 + cc

            @block.sync
            def _(s, og=og, cc=cc, i=i):
                if og == 0 and cc == 0:
                    s.wait_ge(c.S["cfw"], 1)
                s.wait_ge(c.S["sp"], c.wpm.get(i - 4, c.wpm_e))
                src = c.wfg[og * D + cc * 128: og * D + (cc + 1) * 128, 0:512]
                ins = s.dma_start(out=c.wbf[i % 4][:, :], in_=src)
                c.wdm[i] = (f"wb{i % 4}", _mk(c, f"wb{i % 4}", ins, 16))

            @block.tensor
            def _(te, og=og, cc=cc, i=i):
                te.wait_ge(c.S[c.wdm[i][0]], c.wdm[i][1])
                if og == 0 and cc == 0:
                    te.wait_ge(c.S["sv"], c.N["sv"])
                if cc == 0:
                    for g in range(8):
                        te.wait_ge(c.S["sv"], c.evm.get(g, 0))
                last = None
                for g in range(8):
                    last = te.matmul(
                        c.ps[g][:, :],
                        c.arena[:, cc * TPC + g * 128: cc * TPC + (g + 1) * 128],
                        c.wbf[i % 4][:, :], start=(cc == 0), stop=(cc == CC - 1))
                c.wpm[i] = _mk(c, "sp", last)

        _evac_residual(c, og, xin, xout, scale_bias=True, final_out=final_out)


def _p7_attnb(c):
    block = c.block
    sig, retr = _p7_views(c)

    c.wsm, c.wdm, c.wpm = {}, {}, {}
    c.wsm_e = ("sg", c.N["sg"])
    c.wpm_e = c.N["sp"]
    for h8 in range(8):
        for cq in range(8):
            i = h8 * 8 + cq

            @block.sync
            def _(s, h8=h8, cq=cq, i=i):
                w = c.wsm.get(i - 3, c.wsm_e)
                s.wait_ge(c.S[w[0]], w[1])
                src = c.wqselT.rearrange("(cq ch p) (h j) -> cq h p ch j",
                                         ch=4, p=128, j=128)[cq, h8]
                ins = s.dma_start(out=c.wst[i % 3][:, :], in_=src)
                c.wdm[i] = (f"ws{i % 3}", _mk(c, f"ws{i % 3}", ins, 16))

            @block.vector
            def _(v, cq=cq, i=i):
                v.wait_ge(c.S[c.wdm[i][0]], c.wdm[i][1])
                v.wait_ge(c.S["sp"], c.wpm.get(i - 4, c.wpm_e))
                c.wsm[i] = ("sv", _mk(c, "sv", v.tensor_copy(c.wbf[i % 4][:, :], c.wst[i % 3][:, :])))

            @block.tensor
            def _(te, cq=cq, i=i):
                te.wait_ge(c.S[c.wsm[i][0]], c.wsm[i][1])
                last = None
                for ch in range(4):
                    cc = cq * 4 + ch
                    for th in range(2):
                        last = te.matmul(
                            c.ps[th][:, :], c.wbf[i % 4][:, ch * 128:(ch + 1) * 128],
                            c.arena[:, cc * TPC + th * 512: cc * TPC + th * 512 + 512],
                            start=(cc == 0), stop=(cc == CC - 1))
                c.wpm[i] = _mk(c, "sp", last)

        mm_done = c.N["sp"]
        for th in range(2):
            @block.vector
            def _(v, h8=h8, th=th, mm_done=mm_done):
                v.wait_ge(c.S["sp"], mm_done)
                _ch(c, v, "sv", v.tensor_scalar(c.ev[0][:, :], c.ps[th][:, :], c.bq_sb[:, h8:h8 + 1],
                                None, ALU.add))
                _mk(c, "sv", v.tensor_scalar_min(c.ev[1][:, :], c.ev[0][:, :], 0.0))

            @block.scalar
            def _(sc):
                sc.wait_ge(c.S["sv"], c.N["sv"])
                _mk(c, "sa", sc.activation(c.ev[2][:, :], c.ev[1][:, :], AF.Exp,
                                           bias=f32(0.0), scale=f32(1.0)))

            @block.vector
            def _(v, h8=h8, th=th):
                v.wait_ge(c.S["sa"], c.N["sa"])
                _ch(c, v, "sv", v.tensor_relu(c.ev[3][:, :], c.ev[0][:, :]))
                _mk(c, "sv", v.tensor_add(
                    sig[:, h8 * TPC + th * 512: h8 * TPC + (th + 1) * 512],
                    c.ev[2][:, :], c.ev[3][:, :]))

    for h8 in range(8):
        for tt in range(8):
            @block.tensor
            def _(te, h8=h8, tt=tt):
                te.wait_ge(c.S["sv"], c.N["sv"])
                _mk(c, "sp", te.matmul(c.ps[4][:, 0:129],
                                       sig[:, h8 * TPC + tt * 128: h8 * TPC + (tt + 1) * 128],
                                       c.Mz_sb[:, h8 * 129:(h8 + 1) * 129],
                                       start=True, stop=True))

            @block.vector
            def _(v, tt=tt):
                v.wait_ge(c.S["sp"], c.N["sp"])
                _ch(c, v, "sv", v.tensor_scalar(c.small[:, 40:41], c.ps[4][:, 128:129], 1e-6, None, ALU.add))
                _ch(c, v, "sv", v.reciprocal(c.small[:, 41:42], c.small[:, 40:41]))
                _mk(c, "sv", v.tensor_scalar(c.bt[0][:, 0:128], c.ps[4][:, 0:128],
                                             c.small[:, 41:42], None, ALU.mult))

            _transpose_batch(c, [
                (c.bt[0][:, 0:128],
                 retr[:, h8 * TPC + tt * 128: h8 * TPC + (tt + 1) * 128])
            ])

    c.wdm, c.wpm = {}, {}
    c.wpm_e = c.N["sp"]
    for og in range(8):
        for c8 in range(8):
            i = og * 8 + c8

            @block.sync
            def _(s, og=og, c8=c8, i=i):
                if og == 0 and c8 == 0:
                    s.wait_ge(c.S["cwo"], 1)
                s.wait_ge(c.S["sp"], c.wpm.get(i - 4, c.wpm_e))
                src = c.wog[c8 * 128:(c8 + 1) * 128, og * 512:(og + 1) * 512]
                ins = s.dma_start(out=c.wbf[i % 4][:, :], in_=src)
                c.wdm[i] = (f"wb{i % 4}", _mk(c, f"wb{i % 4}", ins, 16))

            @block.tensor
            def _(te, og=og, c8=c8, i=i):
                te.wait_ge(c.S[c.wdm[i][0]], c.wdm[i][1])
                if c8 == 0:
                    if og == 0:
                        te.wait_ge(c.S["sv"], c.N["sv"])
                    for g in range(8):
                        te.wait_ge(c.S["sv"], c.evm.get(g, 0))
                last = None
                for g in range(8):
                    last = te.matmul(
                        c.ps[g][:, :],
                        retr[:, c8 * TPC + g * 128: c8 * TPC + (g + 1) * 128],
                        c.wbf[i % 4][:, :], start=(c8 == 0), stop=(c8 == 7))
                c.wpm[i] = _mk(c, "sp", last)

        _evac_residual(c, og, c.x2, c.x3, scale_bias=False, final_out=False)


def _final(c):
    block = c.block

    @block.sync
    def _(s):
        s.wait_ge(c.S["sw"], c.N["sw"])
        s.wait_ge(c.S["pf"], c.N["pf"])
        for nm, (tap, src) in c.taps.items():
            _mk(c, "pf", s.dma_start(out=tap, in_=src), 16)
        s.wait_ge(c.S["pf"], c.N["pf"])
        s.wait_ge(c.S["sx"], c.N["sx"])
        s.wait_ge(c.S["sg"], c.N["sg"])
        s.wait_ge(c.S["sa"], c.N["sa"])
        s.wait_ge(c.S["sv"], c.N["sv"])
        s.wait_ge(c.S["sp"], c.N["sp"])



def kernel(**inputs):
    in_maps = _host_prep(inputs)
    nc = build_program()
    res = run_bass_kernel_spmd(nc, in_maps, core_ids=list(range(NCORES)))
    out = np.empty((B, T, D), np.float32)
    for ci in range(NCORES):
        o = res.results[ci]["out"]
        out[0, ci * CHUNK:(ci + 1) * CHUNK] = o[:CHUNK]
        out[1, (7 - ci) * CHUNK:(8 - ci) * CHUNK] = o[CHUNK:]
    return out


# version 14
# speedup vs baseline: 1.1380x; 1.1380x over previous
import math
from contextlib import ExitStack

import numpy as np
import ml_dtypes

import concourse.bass as bass
import bass_rust
import concourse.mybir as mybir
from concourse.bass_utils import run_bass_kernel_spmd

F32 = mybir.dt.float32
BF16 = mybir.dt.bfloat16
I32 = mybir.dt.int32
AF = mybir.ActivationFunctionType
ALU = mybir.AluOpType
AX = mybir.AxisListType

f32 = np.float32
BF = ml_dtypes.bfloat16

NCORES = 8
B = 2
T = 4096
D = 4096
CHUNK = 512
TPC = 1024
NG = TPC // 128
CC = D // 128
RSQD = 1.0 / math.sqrt(128.0)
DEAD = -300.0

BUILD_PHASES = 8
DEBUG_TAPS = False



def _rope_tables():
    theta = (1.0 / (10000.0 ** (np.arange(0, 128, 2, dtype=np.float32) / np.float32(128.0)))).astype(np.float32)
    seq = np.arange(T, dtype=np.float32)
    idx = np.outer(seq, theta).astype(np.float32)
    idx2 = np.concatenate([idx, idx], axis=-1)
    return np.cos(idx2).astype(np.float32), np.sin(idx2).astype(np.float32)


def _host_prep(inp):
    x = np.asarray(inp["L0_attn_a_x"], np.float32)
    w_attn = np.asarray(inp["L0_attn_a_w_attn"], np.float32)
    w_proj = np.asarray(inp["L0_attn_a_w_proj"], np.float32)
    awte = np.asarray(inp["L0_attn_a_adapter_wte"], np.float32)
    gating = np.asarray(inp["L0_attn_a_gating"], np.float32)
    Wq = np.asarray(inp["L0_attn_b_Wq"], np.float32)
    bq = np.asarray(inp["L0_attn_b_bq"], np.float32)
    Wo = np.asarray(inp["L0_attn_b_Wo"], np.float32)
    M = np.asarray(inp["L0_attn_b_M"], np.float32)
    z = np.asarray(inp["L0_attn_b_z"], np.float32)
    qp = np.asarray(inp["L0_ffn_qweight_packed"], np.int32)
    fscale = np.asarray(inp["L0_ffn_scale"], np.float32).reshape(1, 1)
    fbias = np.asarray(inp["L0_ffn_bias"], np.float32).reshape(1, D)

    wattnT = np.ascontiguousarray(w_attn.T)
    wprojT = np.ascontiguousarray(w_proj.T)
    sel = (np.arange(8)[:, None] * 512 + np.arange(128)[None, :]).reshape(-1)
    wqselT = np.ascontiguousarray(Wq[sel].T)
    bqsel = np.ascontiguousarray(bq[sel].reshape(8, 128).T)
    woT = np.ascontiguousarray(Wo.T)
    awteT = np.ascontiguousarray(awte.T)
    qp2 = qp.reshape(D, D // 2)
    Mz = np.ascontiguousarray(np.concatenate([M, z[:, :, None]], axis=2).reshape(8 * 128, 129))
    cosT, sinT = _rope_tables()

    kj_idx = np.arange(128)[:, None]
    q_idx = np.arange(512)[None, :]
    bandq = np.concatenate(
        [(kj_idx + j * 128 <= q_idx).astype(np.float32) for j in range(4)], axis=1
    ).astype(BF)
    idn = np.eye(128, dtype=BF)

    in_maps = []
    for c in range(NCORES):
        p0, p1 = c, 7 - c
        own = np.ascontiguousarray(np.concatenate(
            [x[0, p0 * CHUNK:(p0 + 1) * CHUNK], x[1, p1 * CHUNK:(p1 + 1) * CHUNK]], axis=0))
        pos = np.concatenate([np.arange(p0 * CHUNK, (p0 + 1) * CHUNK),
                              np.arange(p1 * CHUNK, (p1 + 1) * CHUNK)])
        cost = np.ascontiguousarray(cosT[pos].T)
        sint = sinT[pos].T.copy()
        sint[:64] = -sint[:64]
        sint = np.ascontiguousarray(sint)
        dead = np.zeros((2, 8), np.float32)
        for kt in range(8):
            if kt >= p0:
                dead[0, kt] = DEAD
            if kt >= p1:
                dead[1, kt] = DEAD
        in_maps.append({
            "x_own": own,
            "wattnT": wattnT,
            "wprojT": wprojT,
            "wqselT": wqselT,
            "bqsel": bqsel,
            "woTs": np.ascontiguousarray(woT[4 * c * 128:(4 * c + 4) * 128]),
            "qps": np.ascontiguousarray(qp2[c * 512:(c + 1) * 512]),
            "awteT": awteT,
            "costab": cost,
            "sintabn": sint,
            "bandq": bandq,
            "deadoff": dead,
            "idn": idn,
            "Mz": Mz,
            "gating": np.ascontiguousarray(gating.reshape(1, 1)),
            "fscale": np.ascontiguousarray(fscale),
            "fbias": np.ascontiguousarray(fbias),
        })
    return in_maps



class Ctx:
    pass


def _mk(c, nm, instr, k=1):
    instr.then_inc(c.S[nm], k)
    c.N[nm] += k
    return c.N[nm]


def _ch(c, eng, nm, instr):
    _mk(c, nm, instr)
    eng.wait_ge(c.S[nm], c.N[nm])


def _mkcc(c, nm, instr):
    instr.then_inc(c.S[nm])
    c.N[nm] += 1
    return c.N[nm]


def _PT(c, i):
    bank = c.ps[7] if i == 0 else c.ps[5]
    return bank[:, 0:64].bitcast(BF16)


def _transpose_batch(c, pairs):
    block = c.block

    @block.tensor
    def _(te):
        base_v = c.N["sv"]
        te.wait_ge(c.S["sv"], base_v)
        te.wait_ge(c.S["sa"], c.N["sa"])
        for j, (src, dst) in enumerate(pairs):
            if j >= 2:
                te.wait_ge(c.S["sv"], base_v + j - 1)
            _mk(c, "sp", te.transpose(_PT(c, j % 2), src, c.idn[:, :]))

    @block.vector
    def _(v):
        base = c.N["sp"] - len(pairs)
        for j, (src, dst) in enumerate(pairs):
            v.wait_ge(c.S["sp"], base + j + 1)
            _mk(c, "sv", v.tensor_copy(dst, _PT(c, j % 2)))



def build_program():
    nc = bass.Bass("TRN2", target_bir_lowering=False)
    c = Ctx()
    c.nc = nc

    def din(name, shape, dt=F32):
        return nc.dram_tensor(name, shape, dt, kind="ExternalInput").ap()

    c.x_own = din("x_own", [TPC, D])
    c.wattnT = din("wattnT", [D, 3 * D])
    c.wprojT = din("wprojT", [D, D])
    c.wqselT = din("wqselT", [D, 1024])
    c.bqsel = din("bqsel", [128, 8])
    c.woTs = din("woTs", [512, D])
    c.qps = din("qps", [512, D // 2], I32)
    c.awteT = din("awteT", [D, 10])
    c.costab = din("costab", [128, TPC])
    c.sintabn = din("sintabn", [128, TPC])
    c.bandq = din("bandq", [128, 2048], BF16)
    c.deadoff = din("deadoff", [2, 8])
    c.idn_d = din("idn", [128, 128], BF16)
    c.Mz_d = din("Mz", [1024, 129])
    c.gating_d = din("gating", [1, 1])
    c.fscale_d = din("fscale", [1, 1])
    c.fbias_d = din("fbias", [1, D])

    c.out = nc.dram_tensor("out", [TPC, D], F32, kind="ExternalOutput").ap()

    def scr(name, shape, dt):
        return nc.dram_tensor(name, shape, dt, kind="Internal").ap()

    c.kc = scr("kc", [4128, TPC], BF16)
    c.kg = nc.dram_tensor("kg", [8 * 4128, TPC], BF16, kind="Internal", addr_space="Shared").ap()
    c.vc = scr("vc", [TPC, D], BF16)
    c.vg = nc.dram_tensor("vg", [8 * TPC, D], BF16, kind="Internal", addr_space="Shared").ap()
    c.wfc = scr("wfc", [D, 512], BF16)
    c.wfg = nc.dram_tensor("wfg", [8 * D, 512], BF16, kind="Internal", addr_space="Shared").ap()
    c.woc = scr("woc", [128, D], BF16)
    c.wog = nc.dram_tensor("wog", [1024, D], BF16, kind="Internal", addr_space="Shared").ap()
    c.qst = scr("qst", [D, TPC], BF16)
    c.x1 = scr("x1", [TPC, D], F32)
    c.x2 = scr("x2", [TPC, D], F32)
    c.x3 = scr("x3", [TPC, D], F32)

    c.taps = {}
    if DEBUG_TAPS:
        for nm, src, shape, dt in [
            ("tap_q", c.qst, [D, TPC], BF16),
            ("tap_kc", c.kc, [4128, TPC], BF16),
            ("tap_vc", c.vc, [TPC, D], BF16),
            ("tap_wf", c.wfg, [8 * D, 512], BF16),
            ("tap_wo", c.wog, [1024, D], BF16),
            ("tap_x1", c.x1, [TPC, D], F32),
            ("tap_x2", c.x2, [TPC, D], F32),
            ("tap_x3", c.x3, [TPC, D], F32),
        ]:
            c.taps[nm] = (nc.dram_tensor(nm, shape, dt, kind="ExternalOutput").ap(), src)

    with ExitStack() as st:
        en = st.enter_context

        def sb(name, shape, dt):
            return en(nc.sbuf_tensor(name, shape, dt))

        c.arena = sb("arena", [128, 32 * TPC], BF16)
        c.scr2 = sb("scr2", [128, 16384], BF16)
        c.cos_sb = sb("cos_sb", [128, TPC], F32)
        c.sin_sb = sb("sin_sb", [128, TPC], F32)
        c.band_sb = sb("band_sb", [128, 2048], BF16)
        c.idn = sb("idn_sb", [128, 128], BF16)
        c.dead_sb = sb("dead_sb", [128, 16], F32)
        c.Mz_sb = sb("Mz_sb", [128, 8 * 129], BF16)
        c.bq_sb = sb("bq_sb", [128, 8], F32)
        c.gat_sb = sb("gat_sb", [128, 1], F32)
        c.fsc_sb = sb("fsc_sb", [128, 1], F32)
        c.ak_sb = sb("ak_sb", [128, 320], BF16)
        c.av_sb = sb("av_sb", [10, D], BF16)
        c.awt_sb = sb("awt_sb", [128, 320], BF16)
        c.ones_bf = sb("ones_bf", [128, 1], BF16)
        c.ones_row = sb("ones_row", [1, 128], BF16)
        c.qn_sb = sb("qn_sb", [128, 256], F32)
        c.xt = [sb(f"xt{i}", [128, D], F32) for i in range(2)]
        c.htok = sb("htok", [128, D], BF16)
        c.wst = [sb(f"wst{i}", [128, 512], F32) for i in range(3)]
        c.wbf = [sb(f"wbf{i}", [128, 512], BF16) for i in range(4)]
        c.xs = [sb(f"xs{i}", [128, 512], F32) for i in range(2)]
        c.fb_sb = [sb(f"fb{i}", [128, 512], F32) for i in range(2)]
        c.ev = [sb(f"ev{i}", [128, 512], F32) for i in range(4)]
        c.bt = [sb(f"bt{i}", [128, 512], BF16) for i in range(4)]
        c.small = sb("small", [128, 64], F32)
        c.btab = sb("btab", [128, 10], F32)
        c.nmk = sb("nmk", [128, 2], F32)
        c.cmaxrow = sb("cmaxrow", [1, 64], F32)
        c.qmaxrow = sb("qmaxrow", [1, 64], F32)
        c.rrow = sb("rrow", [1, 512], F32)
        c.ones_row_f = sb("ones_row_f", [1, 128], F32)
        c.cmaxbf = sb("cmaxbf", [1, 64], BF16)
        c.tiny = sb("tiny", [1, 32], F32)
        c.tinybf = sb("tinybf", [1, 32], BF16)
        c.pa_sb = sb("pa_sb", [128, 10], BF16)
        c.paT_sb = sb("paT_sb", [10, 128], BF16)
        c.qp_sb = None
        c.dq_li = None

        c.ps = [en(nc.psum_tensor(f"ps{i}", [128, 512], F32)) for i in range(8)]

        c.S = {}
        c.N = {}
        for nm in ["sw", "sx", "sg", "sv", "sa", "sp", "pf",
                   "cfw", "cwo", "cka", "cva",
                   "ws0", "ws1", "ws2", "wb0", "wb1", "wb2", "wb3",
                   "kv0", "kv1", "lx0", "lx1", "xs0", "xs1", "fb0", "fb1", "qpl"]:
            c.S[nm] = en(nc.semaphore(nm))
            c.N[nm] = 0
        c.block = en(nc.Block())
        c.kvf = [0, 0]
        c.qf = [0, 0]
        c.xt_free_sp = 0
        c.ln_sg = 0
        c.evm = {}

        _emit(c)

    return nc


def _p5_views(c):
    xt0b = c.xt[0][:, :].bitcast(BF16)
    xt1b = c.xt[1][:, :].bitcast(BF16)
    k_sb = [c.scr2[:, 0:4608], xt0b[:, 0:4608]]
    v_sb = [c.scr2[:, 4608:9216], xt1b[:, 0:4608]]
    q_sb = [c.scr2[:, 11264:11776], xt0b[:, 4608:5120]]
    pt_ring = c.scr2[:, 9216:11264]
    rbc = c.scr2[:, 11776:12800].bitcast(F32)
    return k_sb, v_sb, pt_ring, q_sb, rbc


def _p7_views(c):
    return c.scr2[:, 0:8192], c.scr2[:, 8192:16384]



def _emit(c):
    _p0(c)
    if BUILD_PHASES >= 1:
        _p1_wf(c)
    if BUILD_PHASES >= 2:
        _p2_wo(c)
    if BUILD_PHASES >= 3:
        _p3_ln_hfm(c, c.x_own, first_ln=True)
        _p3_qk(c)
        _p3_v(c)
    if BUILD_PHASES >= 4:
        _p4_gather(c)
    if BUILD_PHASES >= 5:
        _p5_attn(c)
        _p5_proj(c)
    if BUILD_PHASES >= 6:
        _p3_ln_hfm(c, c.x1)
        _p6_ffn(c, c.x1, c.x2)
    if BUILD_PHASES >= 7:
        _p3_ln_hfm(c, c.x2)
        _p7_attnb(c)
    if BUILD_PHASES >= 8:
        _p3_ln_hfm(c, c.x3)
        _p6_ffn(c, c.x3, c.out)
    _final(c)


def _p0(c):
    block = c.block

    @block.sync
    def _(s):
        _mk(c, "sx", s.dma_start(out=c.cos_sb[:, :], in_=c.costab), 16)
        _mk(c, "sx", s.dma_start(out=c.sin_sb[:, :], in_=c.sintabn), 16)
        _mk(c, "sx", s.dma_start(out=c.band_sb[:, :], in_=c.bandq), 16)
        _mk(c, "sx", s.dma_start(out=c.idn[:, :], in_=c.idn_d), 16)
        _mk(c, "sx", s.dma_start(
            out=c.dead_sb[:, :],
            in_=c.deadoff.rearrange("a b -> (a b)").unsqueeze(0).partition_broadcast(128).squeeze(1)), 16)
        _mk(c, "sx", s.dma_start(out=c.bq_sb[:, :], in_=c.bqsel), 16)
        _mk(c, "sx", s.dma_start(
            out=c.gat_sb[:, :], in_=c.gating_d.partition_broadcast(128).squeeze(1)), 16)
        _mk(c, "sx", s.dma_start(
            out=c.fsc_sb[:, :], in_=c.fscale_d.partition_broadcast(128).squeeze(1)), 16)
        _mk(c, "sx", s.dma_start(
            out=c.xt[0][:, 0:1032],
            in_=c.Mz_d.rearrange("(a p) m -> p a m", p=128)), 16)
        _mk(c, "sx", s.dma_start(
            out=c.ev[0][:, 0:320],
            in_=c.awteT.rearrange("(cc p) a -> p cc a", p=128)), 16)
    sx_all = c.N["sx"]

    @block.gpsimd
    def _(g):
        g.memset(c.ones_bf[:, :], 1.0)
        g.memset(c.ones_row[:, :], 1.0)
        g.memset(c.ones_row_f[:, :], 1.0)
        _mk(c, "sg", g.engine_nop())

    @block.vector
    def _(v):
        v.wait_ge(c.S["sx"], sx_all)
        v.tensor_copy(c.Mz_sb[:, :], c.xt[0][:, 0:1032])
        _mk(c, "sv", v.tensor_copy(c.awt_sb[:, :], c.ev[0][:, 0:320]))


def _p1_wf(c):
    block = c.block
    dq = c.arena[:, 0:4096]
    c.qp_sb = c.scr2[:, 0:4096].bitcast(I32)
    c.dq_li = c.scr2[:, 4096:8192].bitcast(I32)

    for t in range(4):
        @block.sync
        def _(s, t=t):
            s.wait_ge(c.S["sv"], c.N["sv"])
            _mk(c, "qpl", s.dma_start(out=c.qp_sb[:, :], in_=c.qps[t * 128:(t + 1) * 128, :]), 16)

        @block.vector
        def _(v, t=t, qld=None):
            v.wait_ge(c.S["qpl"], c.N["qpl"])
            if t >= 1:
                v.wait_ge(c.S["sp"], c.N["sp"])
            _ch(c, v, "sv", v.tensor_scalar(c.dq_li[:, :], c.qp_sb[:, :], 15, None, ALU.bitwise_and))
            _ch(c, v, "sv", v.tensor_scalar(dq[:, 0:4096:2], c.dq_li[:, :], -8.0, None, ALU.add))
            _ch(c, v, "sv", v.tensor_scalar(c.dq_li[:, :], c.qp_sb[:, :], 4, 15,
                            ALU.logical_shift_right, ALU.bitwise_and))
            _mk(c, "sv", v.tensor_scalar(dq[:, 1:4096:2], c.dq_li[:, :], -8.0, None, ALU.add))

        @block.tensor
        def _(te, t=t):
            te.wait_ge(c.S["sv"], c.N["sv"])
            te.wait_ge(c.S["sa"], c.N["sa"])
            if t == 0:
                te.wait_ge(c.S["sx"], 16 * 4)

        base_v = c.N["sv"]
        for j in range(32):
            @block.tensor
            def _(te, t=t, j=j, base_v=base_v):
                if j >= 2:
                    te.wait_ge(c.S["sv"], base_v + j - 1)
                _mk(c, "sp", te.transpose(_PT(c, j % 2), dq[:, j * 128:(j + 1) * 128], c.idn[:, :]))

            @block.vector
            def _(v, t=t, j=j):
                v.wait_ge(c.S["sp"], c.N["sp"])
                if j >= 2:
                    v.wait_ge(c.S["sw"], c.N["sw"] - 16)
                _mk(c, "sv", v.tensor_copy(c.bt[2 + (j % 2)][:, 0:128], _PT(c, j % 2)))

            @block.sync
            def _(s, t=t, j=j):
                s.wait_ge(c.S["sv"], c.N["sv"])
                _mk(c, "sw", s.dma_start(
                    out=c.wfc[j * 128:(j + 1) * 128, t * 128:(t + 1) * 128],
                    in_=c.bt[2 + (j % 2)][:, 0:128]), 16)

    @block.gpsimd
    def _(g):
        g.wait_ge(c.S["sw"], c.N["sw"])
        _mkcc(c, "cfw", g.collective_compute(
            "AllGather", ALU.bypass, replica_groups=[list(range(NCORES))],
            ins=[c.wfc.opt()], outs=[c.wfg.opt()]))


def _p2_wo(c):
    block = c.block
    for sf in range(8):
        @block.sync
        def _(s, sf=sf):
            s.wait_ge(c.S["sv"], c.N["sv"])
            for j, (buf, sem) in enumerate(zip([c.wst[0], c.wst[1], c.ev[0], c.ev[1]],
                                               ["ws0", "ws1", "fb0", "fb1"])):
                _mk(c, sem, s.dma_start(
                    out=buf[:, :],
                    in_=c.woTs[j * 128:(j + 1) * 128, sf * 512:(sf + 1) * 512]), 16)
        p2l = [c.N[s2] for s2 in ("ws0", "ws1", "fb0", "fb1")]

        @block.vector
        def _(v, sf=sf, p2l=p2l):
            for s2, val in zip(("ws0", "ws1", "fb0", "fb1"), p2l):
                v.wait_ge(c.S[s2], val)
            v.tensor_add(c.ev[2][:, :], c.wst[0][:, :], c.wst[1][:, :])
            _ch(c, v, "sv", v.tensor_add(c.ev[3][:, :], c.ev[0][:, :], c.ev[1][:, :]))
            _mk(c, "sv", v.tensor_add(c.bt[0][:, :], c.ev[2][:, :], c.ev[3][:, :]))

        @block.sync
        def _(s, sf=sf):
            s.wait_ge(c.S["sv"], c.N["sv"])
            _mk(c, "sw", s.dma_start(
                out=c.woc[:, sf * 512:(sf + 1) * 512], in_=c.bt[0][:, :]), 16)

    @block.gpsimd
    def _(g):
        g.wait_ge(c.S["sw"], c.N["sw"])
        _mkcc(c, "cwo", g.collective_compute(
            "AllGather", ALU.bypass, replica_groups=[list(range(NCORES))],
            ins=[c.woc.opt()], outs=[c.wog.opt()]))


def _p3_ln_hfm(c, xsrc, first_ln=False):
    block = c.block
    sgbase = c.N["sg"]
    c.htok_free_sp = c.N["sp"]

    for g in range(NG):
        xb = c.xt[g % 2]
        col = (g % 2) * 16

        @block.sync
        def _(s, g=g, xb=xb):
            s.wait_ge(c.S["sg"], c.N["sg"])
            if g < 2:
                s.wait_ge(c.S["sp"], c.xt_free_sp)
            if g == 0:
                s.wait_ge(c.S["sw"], c.N["sw"])
                s.wait_ge(c.S["pf"], c.N["pf"])
                if first_ln:
                    s.wait_ge(c.S["sv"], c.N["sv"])
            _mk(c, f"lx{g % 2}", s.dma_start(out=xb[:, :], in_=xsrc[g * 128:(g + 1) * 128, :]), 16)
        xld = c.N[f"lx{g % 2}"]

        @block.vector
        def _(v, g=g, xb=xb, col=col, sgbase=sgbase, xld=xld):
            v.wait_ge(c.S[f"lx{g % 2}"], xld)
            if g >= 2:
                v.wait_ge(c.S["sg"], sgbase + g - 1)
            _mk(c, "sv", v.reduce_sum(c.small[:, col + 0:col + 1], xb[:, :], axis=AX.X))

        @block.scalar
        def _(sc, g=g, xb=xb, col=col, xld=xld):
            sc.wait_ge(c.S[f"lx{g % 2}"], xld)
            sc.wait_ge(c.S["sp"], c.htok_free_sp)
            _mk(c, "sa", sc.activation(c.htok[:, :], xb[:, :], AF.Square,
                                       accum_out=c.small[:, col + 1:col + 2]))

        @block.vector
        def _(v, g=g, col=col):
            v.wait_ge(c.S["sa"], c.N["sa"])
            _ch(c, v, "sv", v.tensor_scalar(c.small[:, col + 2:col + 3], c.small[:, col + 0:col + 1],
                            1.0 / D, None, ALU.mult))
            _ch(c, v, "sv", v.tensor_mul(c.small[:, col + 3:col + 4], c.small[:, col + 2:col + 3],
                         c.small[:, col + 2:col + 3]))
            _mk(c, "sv", v.scalar_tensor_tensor(
                c.small[:, col + 4:col + 5], c.small[:, col + 1:col + 2], 1.0 / D,
                c.small[:, col + 3:col + 4], ALU.mult, ALU.subtract))

        @block.scalar
        def _(sc, g=g, col=col):
            sc.wait_ge(c.S["sv"], c.N["sv"])
            _ch(c, sc, "sa", sc.activation(c.small[:, col + 5:col + 6], c.small[:, col + 4:col + 5],
                          AF.Ln, bias=f32(1e-5), scale=f32(1.0)))
            _mk(c, "sa", sc.activation(c.small[:, col + 6:col + 7], c.small[:, col + 5:col + 6],
                                       AF.Exp, bias=f32(0.0), scale=f32(-0.5)))

        @block.vector
        def _(v, g=g, col=col):
            v.wait_ge(c.S["sa"], c.N["sa"])
            _mk(c, "sv", v.scalar_tensor_tensor(
                c.small[:, col + 7:col + 8], c.small[:, col + 2:col + 3], -1.0,
                c.small[:, col + 6:col + 7], ALU.mult, ALU.mult))

        @block.gpsimd
        def _(gp, g=g, xb=xb, col=col):
            gp.wait_ge(c.S["sv"], c.N["sv"])
            gp.wait_ge(c.S["sp"], c.htok_free_sp)
            _mk(c, "sg", gp.tensor_scalar(
                c.htok[:, :], xb[:, :], c.small[:, col + 6:col + 7],
                c.small[:, col + 7:col + 8], ALU.mult, ALU.add))

        @block.tensor
        def _(te, g=g):
            te.wait_ge(c.S["sg"], c.N["sg"])
        _transpose_batch(c, [
            (c.htok[:, j * 128:(j + 1) * 128],
             c.arena[:, j * TPC + g * 128: j * TPC + (g + 1) * 128])
            for j in range(CC)
        ])
        c.htok_free_sp = c.N["sp"]
    c.ln_sg = c.N["sg"]


def _p3_qk(c):
    block = c.block
    c.wsm, c.wdm, c.wpm = {}, {}, {}
    c.wsm_e = ("sv", c.N["sv"])
    c.wpm_e = c.N["sp"]

    for ot in range(64):
        is_q = ot < 32
        hh = ot if is_q else ot - 32

        for cq in range(8):
            i = ot * 8 + cq

            @block.sync
            def _(s, ot=ot, cq=cq, i=i):
                w = c.wsm.get(i - 3, c.wsm_e)
                s.wait_ge(c.S[w[0]], w[1])
                src = c.wattnT.rearrange("(cq ch p) (ot j) -> cq ot p ch j",
                                         ch=4, p=128, j=128)[cq, ot]
                ins = s.dma_start(out=c.wst[i % 3][:, :], in_=src)
                c.wdm[i] = (f"ws{i % 3}", _mk(c, f"ws{i % 3}", ins, 16))

            @block.vector
            def _(v, cq=cq, i=i):
                v.wait_ge(c.S[c.wdm[i][0]], c.wdm[i][1])
                v.wait_ge(c.S["sp"], c.wpm.get(i - 4, c.wpm_e))
                c.wsm[i] = ("sv", _mk(c, "sv", v.tensor_copy(c.wbf[i % 4][:, :], c.wst[i % 3][:, :])))

            @block.tensor
            def _(te, ot=ot, cq=cq, i=i, is_q=is_q):
                te.wait_ge(c.S[c.wsm[i][0]], c.wsm[i][1])
                if cq == 0:
                    te.wait_ge(c.S["sa"], c.N["sa"])
                last = None
                for ch in range(4):
                    cc = cq * 4 + ch
                    lhs = c.wbf[i % 4][:, ch * 128:(ch + 1) * 128]
                    for th in range(2):
                        last = te.matmul(
                            c.ps[th][:, :], lhs,
                            c.arena[:, cc * TPC + th * 512: cc * TPC + th * 512 + 512],
                            start=(cc == 0), stop=(cc == CC - 1))
                    if not is_q:
                        last = te.matmul(
                            c.ps[2][:, 0:10], lhs, c.awt_sb[:, cc * 10:(cc + 1) * 10],
                            start=(cc == 0), stop=(cc == CC - 1))
                c.wpm[i] = _mk(c, "sp", last)

        mm_done = c.N["sp"]
        for th in range(2):
            @block.vector
            def _(v, ot=ot, th=th, mm_done=mm_done):
                v.wait_ge(c.S["sp"], mm_done)
                if th == 0:
                    v.wait_ge(c.S["sw"], c.N["sw"])
                ps = c.ps[th]
                cs = c.cos_sb[:, th * 512:(th + 1) * 512]
                sn = c.sin_sb[:, th * 512:(th + 1) * 512]
                v.tensor_mul(c.ev[0][:, :], ps[:, :], cs)
                v.tensor_mul(c.ev[1][0:64, :], ps[64:128, :], sn[0:64, :])
                _ch(c, v, "sv", v.tensor_mul(c.ev[1][64:128, :], ps[0:64, :], sn[64:128, :]))
                _ch(c, v, "sv", v.tensor_add(c.bt[th][:, :], c.ev[0][:, :], c.ev[1][:, :]))
                _mk(c, "sv", v.tensor_mul(c.bt[2 + th][:, :], c.bt[th][:, :], c.bt[th][:, :]))

            @block.tensor
            def _(te, th=th):
                te.wait_ge(c.S["sv"], c.N["sv"])
                _mk(c, "sp", te.matmul(c.ps[3][0:1, :], c.ones_bf[:, :],
                                       c.bt[2 + th][:, :], start=True, stop=True))

            @block.vector
            def _(v, th=th, hh=hh, is_q=is_q):
                v.wait_ge(c.S["sp"], c.N["sp"])
                dst = (c.qmaxrow if is_q else c.cmaxrow)[0:1, hh * 2 + th: hh * 2 + th + 1]
                _mk(c, "sv", v.reduce_max(dst, c.ps[3][0:1, :], axis=AX.X))

            @block.sync
            def _(s, ot=ot, th=th, hh=hh, is_q=is_q):
                s.wait_ge(c.S["sv"], c.N["sv"])
                if is_q:
                    _mk(c, "sw", s.dma_start(
                        out=c.qst[ot * 128:(ot + 1) * 128, th * 512:(th + 1) * 512],
                        in_=c.bt[th][:, :]), 16)
                else:
                    _mk(c, "sw", s.dma_start(
                        out=c.kc[hh * 128:(hh + 1) * 128, th * 512:(th + 1) * 512],
                        in_=c.bt[th][:, :]), 16)

        if not is_q:
            @block.scalar
            def _(sc, hh=hh):
                sc.wait_ge(c.S["sp"], c.N["sp"])
                _mk(c, "sa", sc.copy(c.ak_sb[:, hh * 10:(hh + 1) * 10], c.ps[2][:, 0:10]))

    c.qn_ready = c.N["sa"]

    @block.vector
    def _(v):
        _mk(c, "sv", v.tensor_copy(c.cmaxbf[0:1, :], c.cmaxrow[0:1, :]))

    @block.sync
    def _(s):
        s.wait_ge(c.S["sv"], c.N["sv"])
        _mk(c, "sw", s.dma_start(
            out=c.kc[4096:4128, 0:2],
            in_=c.cmaxbf[0:1, :]), 16)


def _p3_v(c):
    block = c.block

    def vsrc(cc, ov):
        return c.wattnT.rearrange("(cc p) o -> cc p o", p=128)[cc][
            :, 2 * D + ov * 512: 2 * D + (ov + 1) * 512]

    c.wsm, c.wdm, c.wpm = {}, {}, {}
    c.wsm_e = ("sg", c.N["sg"])
    c.wpm_e = c.N["sp"]
    for ov in range(8):
        for cc in range(CC):
            i = ov * 64 + cc

            @block.sync
            def _(s, ov=ov, cc=cc, i=i):
                w = c.wsm.get(i - 3, c.wsm_e)
                s.wait_ge(c.S[w[0]], w[1])
                ins = s.dma_start(out=c.wst[i % 3][:, :], in_=vsrc(cc, ov))
                c.wdm[i] = (f"ws{i % 3}", _mk(c, f"ws{i % 3}", ins, 16))

            @block.vector
            def _(v, cc=cc, i=i):
                v.wait_ge(c.S[c.wdm[i][0]], c.wdm[i][1])
                v.wait_ge(c.S["sp"], c.wpm.get(i - 4, c.wpm_e))
                c.wsm[i] = ("sv", _mk(c, "sv", v.tensor_copy(c.wbf[i % 4][:, :], c.wst[i % 3][:, :])))

            @block.tensor
            def _(te, ov=ov, cc=cc, i=i):
                te.wait_ge(c.S[c.wsm[i][0]], c.wsm[i][1])
                last = None
                for g in range(NG):
                    last = te.matmul(
                        c.ps[g][:, :],
                        c.arena[:, cc * TPC + g * 128: cc * TPC + (g + 1) * 128],
                        c.wbf[i % 4][:, :], start=(cc == 0), stop=(cc == CC - 1))
                c.wpm[i] = _mk(c, "sp", last)

        mm_done = c.N["sp"]
        for g in range(NG):
            @block.vector
            def _(v, g=g, mm_done=mm_done):
                v.wait_ge(c.S["sp"], mm_done)
                v.wait_ge(c.S["sw"], c.N["sw"])
                _mk(c, "sv", v.tensor_copy(c.bt[g % 2][:, :], c.ps[g][:, :]))

            @block.sync
            def _(s, ov=ov, g=g):
                s.wait_ge(c.S["sv"], c.N["sv"])
                _mk(c, "sw", s.dma_start(
                    out=c.vc[g * 128:(g + 1) * 128, ov * 512:(ov + 1) * 512],
                    in_=c.bt[g % 2][:, :]), 16)

        for cc in range(CC):
            i = ov * 64 + 32 + cc

            @block.sync
            def _(s, ov=ov, cc=cc, i=i):
                w = c.wsm.get(i - 3, c.wsm_e)
                s.wait_ge(c.S[w[0]], w[1])
                ins = s.dma_start(out=c.wst[i % 3][:, :], in_=vsrc(cc, ov))
                c.wdm[i] = (f"ws{i % 3}", _mk(c, f"ws{i % 3}", ins, 16))

            @block.vector
            def _(v, cc=cc, i=i):
                v.wait_ge(c.S[c.wdm[i][0]], c.wdm[i][1])
                v.wait_ge(c.S["sp"], c.wpm.get(i - 4, c.wpm_e))
                c.wsm[i] = ("sv", _mk(c, "sv", v.tensor_copy(c.wbf[i % 4][:, :], c.wst[i % 3][:, :])))

            @block.tensor
            def _(te, cc=cc, i=i):
                te.wait_ge(c.S[c.wsm[i][0]], c.wsm[i][1])
                ins = te.matmul(c.ps[0][0:10, :], c.awt_sb[:, cc * 10:(cc + 1) * 10],
                                c.wbf[i % 4][:, :], start=(cc == 0), stop=(cc == CC - 1))
                c.wpm[i] = _mk(c, "sp", ins)

        @block.vector
        def _(v, ov=ov):
            v.wait_ge(c.S["sp"], c.N["sp"])
            _mk(c, "sv", v.tensor_copy(c.av_sb[0:10, ov * 512:(ov + 1) * 512],
                                       c.ps[0][0:10, :]))


def _p4_gather(c):
    block = c.block

    @block.gpsimd
    def _(g):
        g.wait_ge(c.S["sw"], c.N["sw"])
        _mkcc(c, "cka", g.collective_compute(
            "AllGather", ALU.bypass, replica_groups=[list(range(NCORES))],
            ins=[c.kc.opt()], outs=[c.kg.opt()]))
        _mkcc(c, "cva", g.collective_compute(
            "AllGather", ALU.bypass, replica_groups=[list(range(NCORES))],
            ins=[c.vc.opt()], outs=[c.vg.opt()]))


def _p5_attn(c):
    block = c.block
    k_sbs, v_sbs, pt_ring, q_sbs, rbc = _p5_views(c)

    first = True
    prev_evac_sv = c.N["sv"]
    den_mark = {}
    pa_exp_mark = c.qn_ready
    c.bcast_done = 0
    for b in range(B):
        for h in range(32):
            par = (b * 32 + h) % 2
            k_sb = k_sbs[par]
            v_sb = v_sbs[par]
            q_sb = q_sbs[par]
            kv_free = c.kvf[par]
            q_free = c.qf[par]
            ring_guard = [den_mark.get(32 + j, 0) for j in range(4)]

            @block.sync
            def _(s, b=b, h=h, first=first, kv_free=kv_free, q_free=q_free,
                  k_sb=k_sb, q_sb=q_sb, par=par):
                if first:
                    s.wait_ge(c.S["cka"], 1)
                if par == 1:
                    s.wait_ge(c.S["sg"], c.ln_sg)
                s.wait_ge(c.S["sp"], kv_free)
                kv = f"kv{par}"
                if b == 0:
                    kgv = c.kg.rearrange("(r q) t -> q r t", q=4128)
                    _mk(c, kv, s.dma_start(
                        out=k_sb[:, 0:4096],
                        in_=kgv[h * 128:(h + 1) * 128, :, 0:512]), 16)
                else:
                    kap = bass_rust.AP(c.kg.tensor, 7 * 4128 * 1024 + h * 128 * 1024 + 512,
                                       [[1024, 128], [-4128 * 1024, 8], [1, 512]])
                    _mk(c, kv, s.dma_start(out=k_sb[:, 0:4096], in_=kap), 16)
                _mk(c, kv, s.dma_start(
                    out=k_sb[:, 8 * 512:9 * 512],
                    in_=c.kc[h * 128:(h + 1) * 128, b * 512:(b + 1) * 512]), 16)
                s.wait_ge(c.S["sp"], q_free)
                _mk(c, kv, s.dma_start(
                    out=q_sb[:, :],
                    in_=c.qst[h * 128:(h + 1) * 128, b * 512:(b + 1) * 512]), 16)
            kvsem = f"kv{par}"

            @block.gpsimd
            def _(g, b=b, h=h, first=first, kv_free=kv_free, v_sb=v_sb, par=par):
                if first:
                    g.wait_ge(c.S["cva"], 1)
                    g.wait_ge(c.S["cka"], 1)
                if par == 1:
                    g.wait_ge(c.S["sg"], c.ln_sg)
                g.wait_ge(c.S["sp"], kv_free)
                kv = f"kv{par}"
                for kt in range(8):
                    r = kt if b == 0 else 7 - kt
                    vsrc = c.vg[r * TPC + b * 512: r * TPC + b * 512 + 512,
                                h * 128:(h + 1) * 128].rearrange("(j p) e -> p j e", p=128)
                    _mk(c, kv, g.dma_start(
                        out=v_sb[:, kt * 512:(kt + 1) * 512], in_=vsrc), 16)
                _mk(c, kv, g.dma_start(
                    out=v_sb[:, 4096:4608],
                    in_=c.vc[b * 512:(b + 1) * 512,
                             h * 128:(h + 1) * 128].rearrange("(j p) e -> p j e", p=128)), 16)
                _mk(c, kv, g.dma_start(
                    out=c.tinybf[0:1, 0:16],
                    in_=c.kg.rearrange("(r q) t -> q r t", q=4128)[4096 + h][:, 0:2]), 16)
            kvload = c.N[kvsem]

            @block.vector
            def _(v, b=b, h=h, kvsem=kvsem, kvload=kvload, bc=c.bcast_done):
                v.wait_ge(c.S[kvsem], kvload)
                v.wait_ge(c.S["sp"], bc)
                _ch(c, v, "sv", v.reduce_max(c.tiny[0:1, 0:1], c.tinybf[0:1, 0:16], axis=AX.X))
                _mk(c, "sv", v.tensor_mul(c.tiny[0:1, 3:4], c.tiny[0:1, 0:1],
                                          c.qmaxrow[0:1, h * 2 + b: h * 2 + b + 1]))

            @block.scalar
            def _(sc):
                sc.wait_ge(c.S["sv"], c.N["sv"])
                _ch(c, sc, "sa", sc.activation(c.tiny[0:1, 4:5], c.tiny[0:1, 3:4], AF.Ln,
                                               bias=f32(0.0), scale=f32(1.0)))
                _mk(c, "sa", sc.activation(c.tiny[0:1, 5:6], c.tiny[0:1, 4:5], AF.Exp,
                                           bias=f32(0.0), scale=f32(0.5)))

            @block.vector
            def _(v):
                v.wait_ge(c.S["sa"], c.N["sa"])
                _ch(c, v, "sv", v.tensor_scalar(c.tiny[0:1, 6:7], c.tiny[0:1, 5:6],
                                                -RSQD, None, ALU.mult))
                _mk(c, "sv", v.tensor_copy(c.tinybf[0:1, 16:17], c.tiny[0:1, 6:7]))

            @block.tensor
            def _(te):
                te.wait_ge(c.S["sv"], c.N["sv"])
                _mk(c, "sp", te.matmul(c.ps[7][:, 256:257], c.ones_row[0:1, :],
                                       c.tinybf[0:1, 16:17], start=True, stop=True))
            c.bcast_done = c.N["sp"]

            @block.vector
            def _(v, b=b):
                v.wait_ge(c.S["sp"], c.N["sp"])
                v.wait_ge(c.S["sa"], c.N["sa"])
                _ch(c, v, "sv", v.tensor_copy(c.nmk[:, 0:1], c.ps[7][:, 256:257]))
                _ch(c, v, "sv", v.tensor_scalar(c.btab[:, 0:8], c.dead_sb[:, b * 8:(b + 1) * 8],
                                                c.nmk[:, 0:1], None, ALU.add))
                _mk(c, "sv", v.tensor_copy(c.btab[:, 8:9], c.nmk[:, 0:1]))
            bias_ready = c.N["sv"]

            score_mark = {}
            sa0 = c.N["sa"]
            sg0 = c.N["sg"]

            @block.tensor
            def _(te, kvsem=kvsem, kvload=kvload, prev_evac_sv=prev_evac_sv, sa0=sa0, sg0=sg0,
                  k_sb=k_sb, v_sb=v_sb, q_sb=q_sb):
                sbank = [0, 1, 5, 6]
                for step in range(40):
                    ss = step
                    if ss < 36:
                        if ss == 0:
                            te.wait_ge(c.S[kvsem], kvload)
                        if ss < 4:
                            te.wait_ge(c.S["sa"], max(0, sa0 - 6))
                        else:
                            te.wait_ge(c.S["sa"], sa0 + ss - 3)
                        ins = te.matmul(c.ps[sbank[ss % 4]][:, :],
                                        k_sb[:, ss * 128:(ss + 1) * 128],
                                        q_sb[:, :], start=True, stop=True)
                        score_mark[ss] = _mk(c, "sp", ins)
                    if step >= 4:
                        sy = step - 4
                        if sy == 0:
                            te.wait_ge(c.S["sv"], prev_evac_sv)
                        te.wait_ge(c.S["sa"], sa0 + sy + 1)
                        if sy >= 32:
                            te.wait_ge(c.S["sg"], sg0 + (sy - 32) + 1)
                        te.matmul(c.ps[2][:, :], v_sb[:, sy * 128:(sy + 1) * 128],
                                  pt_ring[:, (sy % 4) * 512:(sy % 4 + 1) * 512],
                                  start=(sy == 0), stop=(sy == 35))
                        ins = te.matmul(c.ps[4][0:1, :], c.ones_bf[:, :],
                                        pt_ring[:, (sy % 4) * 512:(sy % 4 + 1) * 512],
                                        start=(sy == 0), stop=(sy == 35))
                        den_mark[sy] = _mk(c, "sp", ins)

            @block.scalar
            def _(sc, bias_ready=bias_ready, ring_guard=ring_guard):
                sc.wait_ge(c.S["sv"], bias_ready)
                for ss in range(36):
                    slot = ss // 4
                    col = slot if slot < 8 else 8
                    sc.wait_ge(c.S["sp"], score_mark[ss])
                    if ss >= 4:
                        sc.wait_ge(c.S["sp"], den_mark[ss - 4])
                    elif ring_guard[ss] > 0:
                        sc.wait_ge(c.S["sp"], ring_guard[ss])
                    _mk(c, "sa", sc.activation(
                        pt_ring[:, (ss % 4) * 512:(ss % 4 + 1) * 512],
                        c.ps[[0, 1, 5, 6][ss % 4]][:, :], AF.Exp,
                        bias=c.btab[:, col:col + 1], scale=f32(RSQD)))

            @block.gpsimd
            def _(g, sa0=sa0):
                for ss in range(32, 36):
                    g.wait_ge(c.S["sa"], sa0 + ss + 1)
                    _mk(c, "sg", g.tensor_mul(
                        pt_ring[:, (ss % 4) * 512:(ss % 4 + 1) * 512],
                        pt_ring[:, (ss % 4) * 512:(ss % 4 + 1) * 512],
                        c.band_sb[:, (ss - 32) * 512:(ss - 31) * 512]))

            for qt in range(4):
                @block.tensor
                def _(te, h=h, qt=qt, kvsem=kvsem, kvload=kvload, pa_exp_mark=pa_exp_mark, q_sb=q_sb):
                    te.wait_ge(c.S[kvsem], kvload)
                    te.wait_ge(c.S["sa"], pa_exp_mark)
                    te.wait_ge(c.S["sv"], c.N["sv"])
                    _mk(c, "sp", te.matmul(c.ps[7][:, 0:10], q_sb[:, qt * 128:(qt + 1) * 128],
                                           c.ak_sb[:, h * 10:(h + 1) * 10],
                                           start=True, stop=True))

                @block.vector
                def _(v):
                    v.wait_ge(c.S["sp"], c.N["sp"])
                    _ch(c, v, "sv", v.reduce_max(c.small[:, 32:33], c.ps[7][:, 0:10], axis=AX.X))
                    _mk(c, "sv", v.tensor_scalar(c.small[:, 33:34], c.small[:, 32:33],
                                                 -1.0 / 64.0, None, ALU.mult))

                @block.scalar
                def _(sc):
                    sc.wait_ge(c.S["sv"], c.N["sv"])
                    _mk(c, "sa", sc.activation(c.pa_sb[:, :], c.ps[7][:, 0:10], AF.Exp,
                                               bias=c.small[:, 33:34], scale=f32(1.0 / 64.0),
                                               accum_out=c.small[:, 34:35]))
                pa_exp_mark = c.N["sa"]

                @block.vector
                def _(v):
                    v.wait_ge(c.S["sa"], c.N["sa"])
                    _ch(c, v, "sv", v.reciprocal(c.small[:, 35:36], c.small[:, 34:35]))
                    _mk(c, "sv", v.tensor_scalar(c.small[:, 36:37], c.small[:, 35:36],
                                                 c.gat_sb[:, 0:1], None, ALU.mult))

                @block.gpsimd
                def _(g):
                    g.wait_ge(c.S["sv"], c.N["sv"])
                    _mk(c, "sg", g.tensor_scalar(c.pa_sb[:, :], c.pa_sb[:, :],
                                                 c.small[:, 36:37], None, ALU.mult))

                @block.tensor
                def _(te):
                    te.wait_ge(c.S["sg"], c.N["sg"])
                    te.wait_ge(c.S["sv"], c.N["sv"])
                    _mk(c, "sp", te.transpose(
                        c.ps[7][:, 64:128].bitcast(BF16)[0:10, :], c.pa_sb[:, :], c.idn[:, :]))

                @block.vector
                def _(v):
                    v.wait_ge(c.S["sp"], c.N["sp"])
                    _mk(c, "sv", v.tensor_copy(c.paT_sb[0:10, :],
                                               c.ps[7][:, 64:128].bitcast(BF16)[0:10, :]))

                @block.tensor
                def _(te, h=h, qt=qt):
                    te.wait_ge(c.S["sv"], c.N["sv"])
                    _mk(c, "sp", te.matmul(c.ps[3][:, qt * 128:(qt + 1) * 128],
                                           c.av_sb[0:10, h * 128:(h + 1) * 128],
                                           c.paT_sb[0:10, :], start=True, stop=True))
            c.qf[par] = c.N["sp"]
            c.kvf[par] = den_mark[35]

            @block.vector
            def _(v):
                v.wait_ge(c.S["sp"], c.N["sp"])
                _mk(c, "sv", v.reciprocal(c.rrow[0:1, :], c.ps[4][0:1, 0:512]))

            @block.tensor
            def _(te):
                te.wait_ge(c.S["sv"], c.N["sv"])
                te.wait_ge(c.S["sa"], c.N["sa"])
                _mk(c, "sp", te.matmul(c.ps[7][:, 0:512], c.ones_row_f[0:1, :],
                                       c.rrow[0:1, :], start=True, stop=True))

            @block.vector
            def _(v, b=b, h=h):
                v.wait_ge(c.S["sp"], c.N["sp"])
                _ch(c, v, "sv", v.tensor_copy(rbc[:, :], c.ps[7][:, 0:512]))
                _ch(c, v, "sv", v.tensor_mul(c.ev[0][:, :], c.ps[2][:, :], rbc[:, :]))
                _mk(c, "sv", v.tensor_add(
                    c.arena[:, h * TPC + b * 512: h * TPC + (b + 1) * 512],
                    c.ev[0][:, :], c.ps[3][:, :]))
            prev_evac_sv = c.N["sv"]

            first = False
    c.xt_free_sp = c.N["sp"]


def _p5_proj(c):
    block = c.block

    c.wsm, c.wdm, c.wpm = {}, {}, {}
    c.wsm_e = ("sg", c.N["sg"])
    c.wpm_e = c.N["sp"]
    for po in range(8):
        for cc in range(CC):
            i = po * 32 + cc

            @block.sync
            def _(s, po=po, cc=cc, i=i):
                w = c.wsm.get(i - 3, c.wsm_e)
                s.wait_ge(c.S[w[0]], w[1])
                src = c.wprojT.rearrange("(cc p) o -> cc p o", p=128)[cc][
                    :, po * 512:(po + 1) * 512]
                ins = s.dma_start(out=c.wst[i % 3][:, :], in_=src)
                c.wdm[i] = (f"ws{i % 3}", _mk(c, f"ws{i % 3}", ins, 16))

            @block.vector
            def _(v, cc=cc, i=i):
                v.wait_ge(c.S[c.wdm[i][0]], c.wdm[i][1])
                v.wait_ge(c.S["sp"], c.wpm.get(i - 4, c.wpm_e))
                c.wsm[i] = ("sv", _mk(c, "sv", v.tensor_copy(c.wbf[i % 4][:, :], c.wst[i % 3][:, :])))

            @block.tensor
            def _(te, po=po, cc=cc, i=i):
                te.wait_ge(c.S[c.wsm[i][0]], c.wsm[i][1])
                if cc == 0:
                    if po == 0:
                        te.wait_ge(c.S["sa"], c.N["sa"])
                    for g in range(8):
                        te.wait_ge(c.S["sv"], c.evm.get(g, 0))
                last = None
                for g in range(8):
                    last = te.matmul(
                        c.ps[g][:, :],
                        c.arena[:, cc * TPC + g * 128: cc * TPC + (g + 1) * 128],
                        c.wbf[i % 4][:, :], start=(cc == 0), stop=(cc == CC - 1))
                c.wpm[i] = _mk(c, "sp", last)

        _evac_residual(c, po, c.x_own, c.x1, scale_bias=False, final_out=False)


def _evac_residual(c, og, xin, xout, scale_bias, final_out):
    block = c.block
    mm_done = c.N["sp"]

    fbl = 0
    if scale_bias:
        @block.scalar
        def _(s, og=og):
            s.wait_ge(c.S["sv"], c.N["sv"])
            _mk(c, f"fb{og % 2}", s.dma_start(
                out=c.fb_sb[og % 2][:, :],
                in_=c.fbias_d[0:1, og * 512:(og + 1) * 512].partition_broadcast(128).squeeze(1)), 16)
        fbl = c.N[f"fb{og % 2}"]

    for g in range(8):
        @block.scalar
        def _(s, og=og, g=g):
            s.wait_ge(c.S["pf"], max(0, c.N["pf"] - 16))
            _mk(c, f"xs{g % 2}", s.dma_start(
                out=c.xs[g % 2][:, :],
                in_=xin[g * 128:(g + 1) * 128, og * 512:(og + 1) * 512]), 16)
        xsl = c.N[f"xs{g % 2}"]

        @block.vector
        def _(v, og=og, g=g, mm_done=mm_done, xsl=xsl, fbl=fbl):
            v.wait_ge(c.S["sp"], mm_done)
            v.wait_ge(c.S[f"xs{g % 2}"], xsl)
            if scale_bias:
                v.wait_ge(c.S[f"fb{og % 2}"], fbl)
            v.wait_ge(c.S["pf"], max(0, c.N["pf"] - 16))
            if scale_bias:
                _ch(c, v, "sv", v.scalar_tensor_tensor(c.ev[2 + g % 2][:, :], c.ps[g][:, :],
                                       c.fsc_sb[:, 0:1], c.fb_sb[og % 2][:, :],
                                       ALU.mult, ALU.add))
                c.evm[g] = _mk(c, "sv", v.tensor_add(c.ev[g % 2][:, :], c.ev[2 + g % 2][:, :],
                                          c.xs[g % 2][:, :]))
            else:
                c.evm[g] = _mk(c, "sv", v.tensor_add(c.ev[g % 2][:, :], c.ps[g][:, :],
                                                     c.xs[g % 2][:, :]))

        @block.scalar
        def _(s, og=og, g=g):
            s.wait_ge(c.S["sv"], c.N["sv"])
            _mk(c, "pf", s.dma_start(
                out=xout[g * 128:(g + 1) * 128, og * 512:(og + 1) * 512],
                in_=c.ev[g % 2][:, :]), 16)


def _p6_ffn(c, xin, xout):
    block = c.block
    final_out = xout is c.out

    c.wdm, c.wpm = {}, {}
    c.wpm_e = c.N["sp"]
    for og in range(8):
        for cc in range(CC):
            i = og * 32 + cc

            @block.sync
            def _(s, og=og, cc=cc, i=i):
                if og == 0 and cc == 0:
                    s.wait_ge(c.S["cfw"], 1)
                s.wait_ge(c.S["sp"], c.wpm.get(i - 4, c.wpm_e))
                src = c.wfg[og * D + cc * 128: og * D + (cc + 1) * 128, 0:512]
                ins = s.dma_start(out=c.wbf[i % 4][:, :], in_=src)
                c.wdm[i] = (f"wb{i % 4}", _mk(c, f"wb{i % 4}", ins, 16))

            @block.tensor
            def _(te, og=og, cc=cc, i=i):
                te.wait_ge(c.S[c.wdm[i][0]], c.wdm[i][1])
                if og == 0 and cc == 0:
                    te.wait_ge(c.S["sv"], c.N["sv"])
                if cc == 0:
                    for g in range(8):
                        te.wait_ge(c.S["sv"], c.evm.get(g, 0))
                last = None
                for g in range(8):
                    last = te.matmul(
                        c.ps[g][:, :],
                        c.arena[:, cc * TPC + g * 128: cc * TPC + (g + 1) * 128],
                        c.wbf[i % 4][:, :], start=(cc == 0), stop=(cc == CC - 1))
                c.wpm[i] = _mk(c, "sp", last)

        _evac_residual(c, og, xin, xout, scale_bias=True, final_out=final_out)


def _p7_attnb(c):
    block = c.block
    sig, retr = _p7_views(c)

    c.wsm, c.wdm, c.wpm = {}, {}, {}
    c.wsm_e = ("sg", c.N["sg"])
    c.wpm_e = c.N["sp"]
    for h8 in range(8):
        for cq in range(8):
            i = h8 * 8 + cq

            @block.sync
            def _(s, h8=h8, cq=cq, i=i):
                w = c.wsm.get(i - 3, c.wsm_e)
                s.wait_ge(c.S[w[0]], w[1])
                src = c.wqselT.rearrange("(cq ch p) (h j) -> cq h p ch j",
                                         ch=4, p=128, j=128)[cq, h8]
                ins = s.dma_start(out=c.wst[i % 3][:, :], in_=src)
                c.wdm[i] = (f"ws{i % 3}", _mk(c, f"ws{i % 3}", ins, 16))

            @block.vector
            def _(v, cq=cq, i=i):
                v.wait_ge(c.S[c.wdm[i][0]], c.wdm[i][1])
                v.wait_ge(c.S["sp"], c.wpm.get(i - 4, c.wpm_e))
                c.wsm[i] = ("sv", _mk(c, "sv", v.tensor_copy(c.wbf[i % 4][:, :], c.wst[i % 3][:, :])))

            @block.tensor
            def _(te, cq=cq, i=i):
                te.wait_ge(c.S[c.wsm[i][0]], c.wsm[i][1])
                last = None
                for ch in range(4):
                    cc = cq * 4 + ch
                    for th in range(2):
                        last = te.matmul(
                            c.ps[th][:, :], c.wbf[i % 4][:, ch * 128:(ch + 1) * 128],
                            c.arena[:, cc * TPC + th * 512: cc * TPC + th * 512 + 512],
                            start=(cc == 0), stop=(cc == CC - 1))
                c.wpm[i] = _mk(c, "sp", last)

        mm_done = c.N["sp"]
        for th in range(2):
            @block.vector
            def _(v, h8=h8, th=th, mm_done=mm_done):
                v.wait_ge(c.S["sp"], mm_done)
                _ch(c, v, "sv", v.tensor_scalar(c.ev[0][:, :], c.ps[th][:, :], c.bq_sb[:, h8:h8 + 1],
                                None, ALU.add))
                _mk(c, "sv", v.tensor_scalar_min(c.ev[1][:, :], c.ev[0][:, :], 0.0))

            @block.scalar
            def _(sc):
                sc.wait_ge(c.S["sv"], c.N["sv"])
                _mk(c, "sa", sc.activation(c.ev[2][:, :], c.ev[1][:, :], AF.Exp,
                                           bias=f32(0.0), scale=f32(1.0)))

            @block.vector
            def _(v, h8=h8, th=th):
                v.wait_ge(c.S["sa"], c.N["sa"])
                _ch(c, v, "sv", v.tensor_relu(c.ev[3][:, :], c.ev[0][:, :]))
                _mk(c, "sv", v.tensor_add(
                    sig[:, h8 * TPC + th * 512: h8 * TPC + (th + 1) * 512],
                    c.ev[2][:, :], c.ev[3][:, :]))

    for h8 in range(8):
        for tt in range(8):
            @block.tensor
            def _(te, h8=h8, tt=tt):
                te.wait_ge(c.S["sv"], c.N["sv"])
                _mk(c, "sp", te.matmul(c.ps[4][:, 0:129],
                                       sig[:, h8 * TPC + tt * 128: h8 * TPC + (tt + 1) * 128],
                                       c.Mz_sb[:, h8 * 129:(h8 + 1) * 129],
                                       start=True, stop=True))

            @block.vector
            def _(v, tt=tt):
                v.wait_ge(c.S["sp"], c.N["sp"])
                _ch(c, v, "sv", v.tensor_scalar(c.small[:, 40:41], c.ps[4][:, 128:129], 1e-6, None, ALU.add))
                _ch(c, v, "sv", v.reciprocal(c.small[:, 41:42], c.small[:, 40:41]))
                _mk(c, "sv", v.tensor_scalar(c.bt[0][:, 0:128], c.ps[4][:, 0:128],
                                             c.small[:, 41:42], None, ALU.mult))

            _transpose_batch(c, [
                (c.bt[0][:, 0:128],
                 retr[:, h8 * TPC + tt * 128: h8 * TPC + (tt + 1) * 128])
            ])

    c.wdm, c.wpm = {}, {}
    c.wpm_e = c.N["sp"]
    for og in range(8):
        for c8 in range(8):
            i = og * 8 + c8

            @block.sync
            def _(s, og=og, c8=c8, i=i):
                if og == 0 and c8 == 0:
                    s.wait_ge(c.S["cwo"], 1)
                s.wait_ge(c.S["sp"], c.wpm.get(i - 4, c.wpm_e))
                src = c.wog[c8 * 128:(c8 + 1) * 128, og * 512:(og + 1) * 512]
                ins = s.dma_start(out=c.wbf[i % 4][:, :], in_=src)
                c.wdm[i] = (f"wb{i % 4}", _mk(c, f"wb{i % 4}", ins, 16))

            @block.tensor
            def _(te, og=og, c8=c8, i=i):
                te.wait_ge(c.S[c.wdm[i][0]], c.wdm[i][1])
                if c8 == 0:
                    if og == 0:
                        te.wait_ge(c.S["sv"], c.N["sv"])
                    for g in range(8):
                        te.wait_ge(c.S["sv"], c.evm.get(g, 0))
                last = None
                for g in range(8):
                    last = te.matmul(
                        c.ps[g][:, :],
                        retr[:, c8 * TPC + g * 128: c8 * TPC + (g + 1) * 128],
                        c.wbf[i % 4][:, :], start=(c8 == 0), stop=(c8 == 7))
                c.wpm[i] = _mk(c, "sp", last)

        _evac_residual(c, og, c.x2, c.x3, scale_bias=False, final_out=False)


def _final(c):
    block = c.block

    @block.sync
    def _(s):
        s.wait_ge(c.S["sw"], c.N["sw"])
        s.wait_ge(c.S["pf"], c.N["pf"])
        for nm, (tap, src) in c.taps.items():
            _mk(c, "pf", s.dma_start(out=tap, in_=src), 16)
        s.wait_ge(c.S["pf"], c.N["pf"])
        s.wait_ge(c.S["sx"], c.N["sx"])
        s.wait_ge(c.S["sg"], c.N["sg"])
        s.wait_ge(c.S["sa"], c.N["sa"])
        s.wait_ge(c.S["sv"], c.N["sv"])
        s.wait_ge(c.S["sp"], c.N["sp"])



def kernel(**inputs):
    in_maps = _host_prep(inputs)
    nc = build_program()
    res = run_bass_kernel_spmd(nc, in_maps, core_ids=list(range(NCORES)))
    out = np.empty((B, T, D), np.float32)
    for ci in range(NCORES):
        o = res.results[ci]["out"]
        out[0, ci * CHUNK:(ci + 1) * CHUNK] = o[:CHUNK]
        out[1, (7 - ci) * CHUNK:(8 - ci) * CHUNK] = o[CHUNK:]
    return out


# version 15
# speedup vs baseline: 1.2406x; 1.0902x over previous
import math
from contextlib import ExitStack

import numpy as np
import ml_dtypes

import concourse.bass as bass
import bass_rust
import concourse.mybir as mybir
from concourse.bass_utils import run_bass_kernel_spmd

F32 = mybir.dt.float32
BF16 = mybir.dt.bfloat16
I32 = mybir.dt.int32
AF = mybir.ActivationFunctionType
ALU = mybir.AluOpType
AX = mybir.AxisListType

f32 = np.float32
BF = ml_dtypes.bfloat16

NCORES = 8
B = 2
T = 4096
D = 4096
CHUNK = 512
TPC = 1024
NG = TPC // 128
CC = D // 128
RSQD = 1.0 / math.sqrt(128.0)
DEAD = -300.0

BUILD_PHASES = 8
DEBUG_TAPS = False



def _rope_tables():
    theta = (1.0 / (10000.0 ** (np.arange(0, 128, 2, dtype=np.float32) / np.float32(128.0)))).astype(np.float32)
    seq = np.arange(T, dtype=np.float32)
    idx = np.outer(seq, theta).astype(np.float32)
    idx2 = np.concatenate([idx, idx], axis=-1)
    return np.cos(idx2).astype(np.float32), np.sin(idx2).astype(np.float32)


def _host_prep(inp):
    x = np.asarray(inp["L0_attn_a_x"], np.float32)
    w_attn = np.asarray(inp["L0_attn_a_w_attn"], np.float32)
    w_proj = np.asarray(inp["L0_attn_a_w_proj"], np.float32)
    awte = np.asarray(inp["L0_attn_a_adapter_wte"], np.float32)
    gating = np.asarray(inp["L0_attn_a_gating"], np.float32)
    Wq = np.asarray(inp["L0_attn_b_Wq"], np.float32)
    bq = np.asarray(inp["L0_attn_b_bq"], np.float32)
    Wo = np.asarray(inp["L0_attn_b_Wo"], np.float32)
    M = np.asarray(inp["L0_attn_b_M"], np.float32)
    z = np.asarray(inp["L0_attn_b_z"], np.float32)
    qp = np.asarray(inp["L0_ffn_qweight_packed"], np.int32)
    fscale = np.asarray(inp["L0_ffn_scale"], np.float32).reshape(1, 1)
    fbias = np.asarray(inp["L0_ffn_bias"], np.float32).reshape(1, D)

    wattnT = np.ascontiguousarray(w_attn.T)
    wprojT = np.ascontiguousarray(w_proj.T)
    sel = (np.arange(8)[:, None] * 512 + np.arange(128)[None, :]).reshape(-1)
    wqselT = np.ascontiguousarray(Wq[sel].T)
    bqsel = np.ascontiguousarray(bq[sel].reshape(8, 128).T)
    woT = np.ascontiguousarray(Wo.T)
    awteT = np.ascontiguousarray(awte.T)
    qp2 = qp.reshape(D, D // 2)
    Mz = np.ascontiguousarray(np.concatenate([M, z[:, :, None]], axis=2).reshape(8 * 128, 129))
    cosT, sinT = _rope_tables()

    kj_idx = np.arange(128)[:, None]
    q_idx = np.arange(512)[None, :]
    bandq = np.concatenate(
        [(kj_idx + j * 128 <= q_idx).astype(np.float32) for j in range(4)], axis=1
    ).astype(BF)
    idn = np.eye(128, dtype=BF)

    in_maps = []
    for c in range(NCORES):
        p0, p1 = c, 7 - c
        own = np.ascontiguousarray(np.concatenate(
            [x[0, p0 * CHUNK:(p0 + 1) * CHUNK], x[1, p1 * CHUNK:(p1 + 1) * CHUNK]], axis=0))
        pos = np.concatenate([np.arange(p0 * CHUNK, (p0 + 1) * CHUNK),
                              np.arange(p1 * CHUNK, (p1 + 1) * CHUNK)])
        cost = np.ascontiguousarray(cosT[pos].T)
        sint = sinT[pos].T.copy()
        sint[:64] = -sint[:64]
        sint = np.ascontiguousarray(sint)
        dead = np.zeros((2, 8), np.float32)
        for kt in range(8):
            if kt >= p0:
                dead[0, kt] = DEAD
            if kt >= p1:
                dead[1, kt] = DEAD
        in_maps.append({
            "x_own": own,
            "wattnT": wattnT,
            "wprojT": wprojT,
            "wqselT": wqselT,
            "bqsel": bqsel,
            "woTs": np.ascontiguousarray(woT[4 * c * 128:(4 * c + 4) * 128]),
            "qps": np.ascontiguousarray(qp2[c * 512:(c + 1) * 512]),
            "awteT": awteT,
            "costab": cost,
            "sintabn": sint,
            "bandq": bandq,
            "deadoff": dead,
            "idn": idn,
            "Mz": Mz,
            "gating": np.ascontiguousarray(gating.reshape(1, 1)),
            "fscale": np.ascontiguousarray(fscale),
            "fbias": np.ascontiguousarray(fbias),
        })
    return in_maps



class Ctx:
    pass


def _mk(c, nm, instr, k=1):
    instr.then_inc(c.S[nm], k)
    c.N[nm] += k
    return c.N[nm]


def _ch(c, eng, nm, instr):
    _mk(c, nm, instr)
    eng.wait_ge(c.S[nm], c.N[nm])


def _mkcc(c, nm, instr):
    instr.then_inc(c.S[nm])
    c.N[nm] += 1
    return c.N[nm]


def _PT(c, i):
    bank = c.ps[7] if i == 0 else c.ps[5]
    return bank[:, 0:64].bitcast(BF16)


def _transpose_batch(c, pairs):
    block = c.block

    @block.tensor
    def _(te):
        base_v = c.N["sv"]
        te.wait_ge(c.S["sv"], base_v)
        te.wait_ge(c.S["sa"], c.N["sa"])
        for j, (src, dst) in enumerate(pairs):
            if j >= 2:
                te.wait_ge(c.S["sv"], base_v + j - 1)
            _mk(c, "sp", te.transpose(_PT(c, j % 2), src, c.idn[:, :]))

    @block.vector
    def _(v):
        base = c.N["sp"] - len(pairs)
        for j, (src, dst) in enumerate(pairs):
            v.wait_ge(c.S["sp"], base + j + 1)
            _mk(c, "sv", v.tensor_copy(dst, _PT(c, j % 2)))



def build_program():
    nc = bass.Bass("TRN2", target_bir_lowering=False)
    c = Ctx()
    c.nc = nc

    def din(name, shape, dt=F32):
        return nc.dram_tensor(name, shape, dt, kind="ExternalInput").ap()

    c.x_own = din("x_own", [TPC, D])
    c.wattnT = din("wattnT", [D, 3 * D])
    c.wprojT = din("wprojT", [D, D])
    c.wqselT = din("wqselT", [D, 1024])
    c.bqsel = din("bqsel", [128, 8])
    c.woTs = din("woTs", [512, D])
    c.qps = din("qps", [512, D // 2], I32)
    c.awteT = din("awteT", [D, 10])
    c.costab = din("costab", [128, TPC])
    c.sintabn = din("sintabn", [128, TPC])
    c.bandq = din("bandq", [128, 2048], BF16)
    c.deadoff = din("deadoff", [2, 8])
    c.idn_d = din("idn", [128, 128], BF16)
    c.Mz_d = din("Mz", [1024, 129])
    c.gating_d = din("gating", [1, 1])
    c.fscale_d = din("fscale", [1, 1])
    c.fbias_d = din("fbias", [1, D])

    c.out = nc.dram_tensor("out", [TPC, D], F32, kind="ExternalOutput").ap()

    def scr(name, shape, dt):
        return nc.dram_tensor(name, shape, dt, kind="Internal").ap()

    c.kc = scr("kc", [4128, TPC], BF16)
    c.kg = nc.dram_tensor("kg", [8 * 4128, TPC], BF16, kind="Internal", addr_space="Shared").ap()
    c.vc = scr("vc", [TPC, D], BF16)
    c.vg = nc.dram_tensor("vg", [8 * TPC, D], BF16, kind="Internal", addr_space="Shared").ap()
    c.wfc = scr("wfc", [D, 512], BF16)
    c.wfg = nc.dram_tensor("wfg", [8 * D, 512], BF16, kind="Internal", addr_space="Shared").ap()
    c.woc = scr("woc", [128, D], BF16)
    c.wog = nc.dram_tensor("wog", [1024, D], BF16, kind="Internal", addr_space="Shared").ap()
    c.qst = scr("qst", [D, TPC], BF16)
    c.x1 = scr("x1", [TPC, D], F32)
    c.x2 = scr("x2", [TPC, D], F32)
    c.x3 = scr("x3", [TPC, D], F32)

    c.taps = {}
    if DEBUG_TAPS:
        for nm, src, shape, dt in [
            ("tap_q", c.qst, [D, TPC], BF16),
            ("tap_kc", c.kc, [4128, TPC], BF16),
            ("tap_vc", c.vc, [TPC, D], BF16),
            ("tap_wf", c.wfg, [8 * D, 512], BF16),
            ("tap_wo", c.wog, [1024, D], BF16),
            ("tap_x1", c.x1, [TPC, D], F32),
            ("tap_x2", c.x2, [TPC, D], F32),
            ("tap_x3", c.x3, [TPC, D], F32),
        ]:
            c.taps[nm] = (nc.dram_tensor(nm, shape, dt, kind="ExternalOutput").ap(), src)

    with ExitStack() as st:
        en = st.enter_context

        def sb(name, shape, dt):
            return en(nc.sbuf_tensor(name, shape, dt))

        c.arena = sb("arena", [128, 32 * TPC], BF16)
        c.scr2 = sb("scr2", [128, 16384], BF16)
        c.cos_sb = sb("cos_sb", [128, TPC], F32)
        c.sin_sb = sb("sin_sb", [128, TPC], F32)
        c.band_sb = sb("band_sb", [128, 2048], BF16)
        c.idn = sb("idn_sb", [128, 128], BF16)
        c.dead_sb = sb("dead_sb", [128, 16], F32)
        c.Mz_sb = sb("Mz_sb", [128, 8 * 129], BF16)
        c.bq_sb = sb("bq_sb", [128, 8], F32)
        c.gat_sb = sb("gat_sb", [128, 1], F32)
        c.fsc_sb = sb("fsc_sb", [128, 1], F32)
        c.ak_sb = sb("ak_sb", [128, 320], BF16)
        c.av_sb = sb("av_sb", [10, D], BF16)
        c.awt_sb = sb("awt_sb", [128, 320], BF16)
        c.ones_bf = sb("ones_bf", [128, 1], BF16)
        c.ones_row = sb("ones_row", [1, 128], BF16)
        c.qn_sb = sb("qn_sb", [128, 256], F32)
        c.xt = [sb(f"xt{i}", [128, D], F32) for i in range(2)]
        c.htok = sb("htok", [128, D], BF16)
        c.wst = [sb(f"wst{i}", [128, 512], F32) for i in range(3)]
        c.wbf = [sb(f"wbf{i}", [128, 512], BF16) for i in range(4)]
        c.xs = [sb(f"xs{i}", [128, 512], F32) for i in range(2)]
        c.fb_sb = [sb(f"fb{i}", [128, 512], F32) for i in range(2)]
        c.ev = [sb(f"ev{i}", [128, 512], F32) for i in range(4)]
        c.bt = [sb(f"bt{i}", [128, 512], BF16) for i in range(4)]
        c.small = sb("small", [128, 64], F32)
        c.btab = sb("btab", [128, 10], F32)
        c.nmk = sb("nmk", [128, 2], F32)
        c.cmaxrow = sb("cmaxrow", [1, 64], F32)
        c.qmaxrow = sb("qmaxrow", [1, 64], F32)
        c.akmaxrow = sb("akmaxrow", [1, 32], F32)
        c.aksq = sb("aksq", [128, 10], BF16)
        c.rrow = sb("rrow", [1, 512], F32)
        c.ones_row_f = sb("ones_row_f", [1, 128], F32)
        c.cmaxbf = sb("cmaxbf", [1, 64], BF16)
        c.tiny = sb("tiny", [1, 32], F32)
        c.tinybf = sb("tinybf", [1, 32], BF16)
        c.pa_sb = sb("pa_sb", [128, 40], BF16)
        c.paT_sb = sb("paT_sb", [10, 512], BF16)
        c.qp_sb = None
        c.dq_li = None

        c.ps = [en(nc.psum_tensor(f"ps{i}", [128, 512], F32)) for i in range(8)]

        c.S = {}
        c.N = {}
        for nm in ["sw", "sx", "sg", "sv", "sa", "sp", "pf",
                   "cfw", "cwo", "cka", "cva",
                   "ws0", "ws1", "ws2", "wb0", "wb1", "wb2", "wb3",
                   "kv0", "kv1", "lx0", "lx1", "xs0", "xs1", "fb0", "fb1", "qpl"]:
            c.S[nm] = en(nc.semaphore(nm))
            c.N[nm] = 0
        c.block = en(nc.Block())
        c.kvf = [0, 0]
        c.qf = [0, 0]
        c.xt_free_sp = 0
        c.ln_sg = 0
        c.evm = {}

        _emit(c)

    return nc


def _p5_views(c):
    xt0b = c.xt[0][:, :].bitcast(BF16)
    xt1b = c.xt[1][:, :].bitcast(BF16)
    k_sb = [c.scr2[:, 0:4608], xt0b[:, 0:4608]]
    v_sb = [c.scr2[:, 4608:9216], xt1b[:, 0:4608]]
    q_sb = [c.scr2[:, 11264:11776], xt0b[:, 4608:5120]]
    pt_ring = c.scr2[:, 9216:11264]
    rbc = c.scr2[:, 11776:12800].bitcast(F32)
    return k_sb, v_sb, pt_ring, q_sb, rbc


def _p7_views(c):
    return c.scr2[:, 0:8192], c.scr2[:, 8192:16384]



def _emit(c):
    _p0(c)
    if BUILD_PHASES >= 1:
        _p1_wf(c)
    if BUILD_PHASES >= 2:
        _p2_wo(c)
    if BUILD_PHASES >= 3:
        _p3_ln_hfm(c, c.x_own, first_ln=True)
        _p3_qk(c)
        _p3_v(c)
    if BUILD_PHASES >= 4:
        _p4_gather(c)
    if BUILD_PHASES >= 5:
        _p5_attn(c)
        _p5_proj(c)
    if BUILD_PHASES >= 6:
        _p3_ln_hfm(c, c.x1)
        _p6_ffn(c, c.x1, c.x2)
    if BUILD_PHASES >= 7:
        _p3_ln_hfm(c, c.x2)
        _p7_attnb(c)
    if BUILD_PHASES >= 8:
        _p3_ln_hfm(c, c.x3)
        _p6_ffn(c, c.x3, c.out)
    _final(c)


def _p0(c):
    block = c.block

    @block.sync
    def _(s):
        _mk(c, "sx", s.dma_start(out=c.cos_sb[:, :], in_=c.costab), 16)
        _mk(c, "sx", s.dma_start(out=c.sin_sb[:, :], in_=c.sintabn), 16)
        _mk(c, "sx", s.dma_start(out=c.band_sb[:, :], in_=c.bandq), 16)
        _mk(c, "sx", s.dma_start(out=c.idn[:, :], in_=c.idn_d), 16)
        _mk(c, "sx", s.dma_start(
            out=c.dead_sb[:, :],
            in_=c.deadoff.rearrange("a b -> (a b)").unsqueeze(0).partition_broadcast(128).squeeze(1)), 16)
        _mk(c, "sx", s.dma_start(out=c.bq_sb[:, :], in_=c.bqsel), 16)
        _mk(c, "sx", s.dma_start(
            out=c.gat_sb[:, :], in_=c.gating_d.partition_broadcast(128).squeeze(1)), 16)
        _mk(c, "sx", s.dma_start(
            out=c.fsc_sb[:, :], in_=c.fscale_d.partition_broadcast(128).squeeze(1)), 16)
        _mk(c, "sx", s.dma_start(
            out=c.xt[0][:, 0:1032],
            in_=c.Mz_d.rearrange("(a p) m -> p a m", p=128)), 16)
        _mk(c, "sx", s.dma_start(
            out=c.ev[0][:, 0:320],
            in_=c.awteT.rearrange("(cc p) a -> p cc a", p=128)), 16)
    sx_all = c.N["sx"]

    @block.gpsimd
    def _(g):
        g.memset(c.ones_bf[:, :], 1.0)
        g.memset(c.ones_row[:, :], 1.0)
        g.memset(c.ones_row_f[:, :], 1.0)
        _mk(c, "sg", g.engine_nop())

    @block.vector
    def _(v):
        v.wait_ge(c.S["sx"], sx_all)
        v.tensor_copy(c.Mz_sb[:, :], c.xt[0][:, 0:1032])
        _mk(c, "sv", v.tensor_copy(c.awt_sb[:, :], c.ev[0][:, 0:320]))


def _p1_wf(c):
    block = c.block
    dq = c.arena[:, 0:4096]
    c.qp_sb = c.scr2[:, 0:4096].bitcast(I32)
    c.dq_li = c.scr2[:, 4096:8192].bitcast(I32)

    for t in range(4):
        @block.sync
        def _(s, t=t):
            s.wait_ge(c.S["sv"], c.N["sv"])
            _mk(c, "qpl", s.dma_start(out=c.qp_sb[:, :], in_=c.qps[t * 128:(t + 1) * 128, :]), 16)

        @block.vector
        def _(v, t=t, qld=None):
            v.wait_ge(c.S["qpl"], c.N["qpl"])
            if t >= 1:
                v.wait_ge(c.S["sp"], c.N["sp"])
            _ch(c, v, "sv", v.tensor_scalar(c.dq_li[:, :], c.qp_sb[:, :], 15, None, ALU.bitwise_and))
            _ch(c, v, "sv", v.tensor_scalar(dq[:, 0:4096:2], c.dq_li[:, :], -8.0, None, ALU.add))
            _ch(c, v, "sv", v.tensor_scalar(c.dq_li[:, :], c.qp_sb[:, :], 4, 15,
                            ALU.logical_shift_right, ALU.bitwise_and))
            _mk(c, "sv", v.tensor_scalar(dq[:, 1:4096:2], c.dq_li[:, :], -8.0, None, ALU.add))

        @block.tensor
        def _(te, t=t):
            te.wait_ge(c.S["sv"], c.N["sv"])
            te.wait_ge(c.S["sa"], c.N["sa"])
            if t == 0:
                te.wait_ge(c.S["sx"], 16 * 4)

        base_v = c.N["sv"]
        for j in range(32):
            @block.tensor
            def _(te, t=t, j=j, base_v=base_v):
                if j >= 2:
                    te.wait_ge(c.S["sv"], base_v + j - 1)
                _mk(c, "sp", te.transpose(_PT(c, j % 2), dq[:, j * 128:(j + 1) * 128], c.idn[:, :]))

            @block.vector
            def _(v, t=t, j=j):
                v.wait_ge(c.S["sp"], c.N["sp"])
                if j >= 2:
                    v.wait_ge(c.S["sw"], c.N["sw"] - 16)
                _mk(c, "sv", v.tensor_copy(c.bt[2 + (j % 2)][:, 0:128], _PT(c, j % 2)))

            @block.sync
            def _(s, t=t, j=j):
                s.wait_ge(c.S["sv"], c.N["sv"])
                _mk(c, "sw", s.dma_start(
                    out=c.wfc[j * 128:(j + 1) * 128, t * 128:(t + 1) * 128],
                    in_=c.bt[2 + (j % 2)][:, 0:128]), 16)

    @block.gpsimd
    def _(g):
        g.wait_ge(c.S["sw"], c.N["sw"])
        _mkcc(c, "cfw", g.collective_compute(
            "AllGather", ALU.bypass, replica_groups=[list(range(NCORES))],
            ins=[c.wfc.opt()], outs=[c.wfg.opt()]))


def _p2_wo(c):
    block = c.block
    for sf in range(8):
        @block.sync
        def _(s, sf=sf):
            s.wait_ge(c.S["sv"], c.N["sv"])
            for j, (buf, sem) in enumerate(zip([c.wst[0], c.wst[1], c.ev[0], c.ev[1]],
                                               ["ws0", "ws1", "fb0", "fb1"])):
                _mk(c, sem, s.dma_start(
                    out=buf[:, :],
                    in_=c.woTs[j * 128:(j + 1) * 128, sf * 512:(sf + 1) * 512]), 16)
        p2l = [c.N[s2] for s2 in ("ws0", "ws1", "fb0", "fb1")]

        @block.vector
        def _(v, sf=sf, p2l=p2l):
            for s2, val in zip(("ws0", "ws1", "fb0", "fb1"), p2l):
                v.wait_ge(c.S[s2], val)
            v.tensor_add(c.ev[2][:, :], c.wst[0][:, :], c.wst[1][:, :])
            _ch(c, v, "sv", v.tensor_add(c.ev[3][:, :], c.ev[0][:, :], c.ev[1][:, :]))
            _mk(c, "sv", v.tensor_add(c.bt[0][:, :], c.ev[2][:, :], c.ev[3][:, :]))

        @block.sync
        def _(s, sf=sf):
            s.wait_ge(c.S["sv"], c.N["sv"])
            _mk(c, "sw", s.dma_start(
                out=c.woc[:, sf * 512:(sf + 1) * 512], in_=c.bt[0][:, :]), 16)

    @block.gpsimd
    def _(g):
        g.wait_ge(c.S["sw"], c.N["sw"])
        _mkcc(c, "cwo", g.collective_compute(
            "AllGather", ALU.bypass, replica_groups=[list(range(NCORES))],
            ins=[c.woc.opt()], outs=[c.wog.opt()]))


def _p3_ln_hfm(c, xsrc, first_ln=False):
    block = c.block
    sgbase = c.N["sg"]
    c.htok_free_sp = c.N["sp"]

    for g in range(NG):
        xb = c.xt[g % 2]
        col = (g % 2) * 16

        @block.sync
        def _(s, g=g, xb=xb):
            s.wait_ge(c.S["sg"], c.N["sg"])
            if g < 2:
                s.wait_ge(c.S["sp"], c.xt_free_sp)
            if g == 0:
                s.wait_ge(c.S["sw"], c.N["sw"])
                s.wait_ge(c.S["pf"], c.N["pf"])
                if first_ln:
                    s.wait_ge(c.S["sv"], c.N["sv"])
            _mk(c, f"lx{g % 2}", s.dma_start(out=xb[:, :], in_=xsrc[g * 128:(g + 1) * 128, :]), 16)
        xld = c.N[f"lx{g % 2}"]

        @block.vector
        def _(v, g=g, xb=xb, col=col, sgbase=sgbase, xld=xld):
            v.wait_ge(c.S[f"lx{g % 2}"], xld)
            if g >= 2:
                v.wait_ge(c.S["sg"], sgbase + g - 1)
            _mk(c, "sv", v.reduce_sum(c.small[:, col + 0:col + 1], xb[:, :], axis=AX.X))

        @block.scalar
        def _(sc, g=g, xb=xb, col=col, xld=xld):
            sc.wait_ge(c.S[f"lx{g % 2}"], xld)
            sc.wait_ge(c.S["sp"], c.htok_free_sp)
            _mk(c, "sa", sc.activation(c.htok[:, :], xb[:, :], AF.Square,
                                       accum_out=c.small[:, col + 1:col + 2]))

        @block.vector
        def _(v, g=g, col=col):
            v.wait_ge(c.S["sa"], c.N["sa"])
            _ch(c, v, "sv", v.tensor_scalar(c.small[:, col + 2:col + 3], c.small[:, col + 0:col + 1],
                            1.0 / D, None, ALU.mult))
            _ch(c, v, "sv", v.tensor_mul(c.small[:, col + 3:col + 4], c.small[:, col + 2:col + 3],
                         c.small[:, col + 2:col + 3]))
            _mk(c, "sv", v.scalar_tensor_tensor(
                c.small[:, col + 4:col + 5], c.small[:, col + 1:col + 2], 1.0 / D,
                c.small[:, col + 3:col + 4], ALU.mult, ALU.subtract))

        @block.scalar
        def _(sc, g=g, col=col):
            sc.wait_ge(c.S["sv"], c.N["sv"])
            _ch(c, sc, "sa", sc.activation(c.small[:, col + 5:col + 6], c.small[:, col + 4:col + 5],
                          AF.Ln, bias=f32(1e-5), scale=f32(1.0)))
            _mk(c, "sa", sc.activation(c.small[:, col + 6:col + 7], c.small[:, col + 5:col + 6],
                                       AF.Exp, bias=f32(0.0), scale=f32(-0.5)))

        @block.vector
        def _(v, g=g, col=col):
            v.wait_ge(c.S["sa"], c.N["sa"])
            _mk(c, "sv", v.scalar_tensor_tensor(
                c.small[:, col + 7:col + 8], c.small[:, col + 2:col + 3], -1.0,
                c.small[:, col + 6:col + 7], ALU.mult, ALU.mult))

        @block.gpsimd
        def _(gp, g=g, xb=xb, col=col):
            gp.wait_ge(c.S["sv"], c.N["sv"])
            gp.wait_ge(c.S["sp"], c.htok_free_sp)
            _mk(c, "sg", gp.tensor_scalar(
                c.htok[:, :], xb[:, :], c.small[:, col + 6:col + 7],
                c.small[:, col + 7:col + 8], ALU.mult, ALU.add))

        @block.tensor
        def _(te, g=g):
            te.wait_ge(c.S["sg"], c.N["sg"])
        _transpose_batch(c, [
            (c.htok[:, j * 128:(j + 1) * 128],
             c.arena[:, j * TPC + g * 128: j * TPC + (g + 1) * 128])
            for j in range(CC)
        ])
        c.htok_free_sp = c.N["sp"]
    c.ln_sg = c.N["sg"]


def _p3_qk(c):
    block = c.block
    c.wsm, c.wdm, c.wpm = {}, {}, {}
    c.wsm_e = ("sv", c.N["sv"])
    c.wpm_e = c.N["sp"]

    for ot in range(64):
        is_q = ot < 32
        hh = ot if is_q else ot - 32

        for cq in range(8):
            i = ot * 8 + cq

            @block.sync
            def _(s, ot=ot, cq=cq, i=i):
                w = c.wsm.get(i - 3, c.wsm_e)
                s.wait_ge(c.S[w[0]], w[1])
                src = c.wattnT.rearrange("(cq ch p) (ot j) -> cq ot p ch j",
                                         ch=4, p=128, j=128)[cq, ot]
                ins = s.dma_start(out=c.wst[i % 3][:, :], in_=src)
                c.wdm[i] = (f"ws{i % 3}", _mk(c, f"ws{i % 3}", ins, 16))

            @block.vector
            def _(v, cq=cq, i=i):
                v.wait_ge(c.S[c.wdm[i][0]], c.wdm[i][1])
                v.wait_ge(c.S["sp"], c.wpm.get(i - 4, c.wpm_e))
                c.wsm[i] = ("sv", _mk(c, "sv", v.tensor_copy(c.wbf[i % 4][:, :], c.wst[i % 3][:, :])))

            @block.tensor
            def _(te, ot=ot, cq=cq, i=i, is_q=is_q):
                te.wait_ge(c.S[c.wsm[i][0]], c.wsm[i][1])
                if cq == 0:
                    te.wait_ge(c.S["sa"], c.N["sa"])
                last = None
                for ch in range(4):
                    cc = cq * 4 + ch
                    lhs = c.wbf[i % 4][:, ch * 128:(ch + 1) * 128]
                    for th in range(2):
                        last = te.matmul(
                            c.ps[th][:, :], lhs,
                            c.arena[:, cc * TPC + th * 512: cc * TPC + th * 512 + 512],
                            start=(cc == 0), stop=(cc == CC - 1))
                    if not is_q:
                        last = te.matmul(
                            c.ps[2][:, 0:10], lhs, c.awt_sb[:, cc * 10:(cc + 1) * 10],
                            start=(cc == 0), stop=(cc == CC - 1))
                c.wpm[i] = _mk(c, "sp", last)

        mm_done = c.N["sp"]
        for th in range(2):
            @block.vector
            def _(v, ot=ot, th=th, mm_done=mm_done):
                v.wait_ge(c.S["sp"], mm_done)
                if th == 0:
                    v.wait_ge(c.S["sw"], c.N["sw"])
                ps = c.ps[th]
                cs = c.cos_sb[:, th * 512:(th + 1) * 512]
                sn = c.sin_sb[:, th * 512:(th + 1) * 512]
                v.tensor_mul(c.ev[0][:, :], ps[:, :], cs)
                v.tensor_mul(c.ev[1][0:64, :], ps[64:128, :], sn[0:64, :])
                _ch(c, v, "sv", v.tensor_mul(c.ev[1][64:128, :], ps[0:64, :], sn[64:128, :]))
                _ch(c, v, "sv", v.tensor_add(c.bt[th][:, :], c.ev[0][:, :], c.ev[1][:, :]))
                _mk(c, "sv", v.tensor_mul(c.bt[2 + th][:, :], c.bt[th][:, :], c.bt[th][:, :]))

            @block.tensor
            def _(te, th=th):
                te.wait_ge(c.S["sv"], c.N["sv"])
                _mk(c, "sp", te.matmul(c.ps[3][0:1, :], c.ones_bf[:, :],
                                       c.bt[2 + th][:, :], start=True, stop=True))

            @block.vector
            def _(v, th=th, hh=hh, is_q=is_q):
                v.wait_ge(c.S["sp"], c.N["sp"])
                dst = (c.qmaxrow if is_q else c.cmaxrow)[0:1, hh * 2 + th: hh * 2 + th + 1]
                _mk(c, "sv", v.reduce_max(dst, c.ps[3][0:1, :], axis=AX.X))

            @block.sync
            def _(s, ot=ot, th=th, hh=hh, is_q=is_q):
                s.wait_ge(c.S["sv"], c.N["sv"])
                if is_q:
                    _mk(c, "sw", s.dma_start(
                        out=c.qst[ot * 128:(ot + 1) * 128, th * 512:(th + 1) * 512],
                        in_=c.bt[th][:, :]), 16)
                else:
                    _mk(c, "sw", s.dma_start(
                        out=c.kc[hh * 128:(hh + 1) * 128, th * 512:(th + 1) * 512],
                        in_=c.bt[th][:, :]), 16)

        if not is_q:
            @block.scalar
            def _(sc, hh=hh):
                sc.wait_ge(c.S["sp"], c.N["sp"])
                _mk(c, "sa", sc.copy(c.ak_sb[:, hh * 10:(hh + 1) * 10], c.ps[2][:, 0:10]))

            @block.vector
            def _(v, hh=hh):
                v.wait_ge(c.S["sa"], c.N["sa"])
                _ch(c, v, "sv", v.tensor_mul(c.aksq[:, 0:10], c.ak_sb[:, hh * 10:(hh + 1) * 10],
                                             c.ak_sb[:, hh * 10:(hh + 1) * 10]))

            @block.tensor
            def _(te, hh=hh):
                te.wait_ge(c.S["sv"], c.N["sv"])
                _mk(c, "sp", te.matmul(c.ps[3][0:1, 256:266], c.ones_bf[:, :],
                                       c.aksq[:, 0:10], start=True, stop=True))

            @block.vector
            def _(v, hh=hh):
                v.wait_ge(c.S["sp"], c.N["sp"])
                _mk(c, "sv", v.reduce_max(c.akmaxrow[0:1, hh:hh + 1],
                                          c.ps[3][0:1, 256:266], axis=AX.X))

    c.qn_ready = c.N["sa"]

    @block.vector
    def _(v):
        _mk(c, "sv", v.tensor_copy(c.cmaxbf[0:1, :], c.cmaxrow[0:1, :]))

    @block.sync
    def _(s):
        s.wait_ge(c.S["sv"], c.N["sv"])
        _mk(c, "sw", s.dma_start(
            out=c.kc[4096:4128, 0:2],
            in_=c.cmaxbf[0:1, :]), 16)


def _p3_v(c):
    block = c.block

    def vsrc(cc, ov):
        return c.wattnT.rearrange("(cc p) o -> cc p o", p=128)[cc][
            :, 2 * D + ov * 512: 2 * D + (ov + 1) * 512]

    c.wsm, c.wdm, c.wpm = {}, {}, {}
    c.wsm_e = ("sg", c.N["sg"])
    c.wpm_e = c.N["sp"]
    for ov in range(8):
        for cc in range(CC):
            i = ov * 64 + cc

            @block.sync
            def _(s, ov=ov, cc=cc, i=i):
                w = c.wsm.get(i - 3, c.wsm_e)
                s.wait_ge(c.S[w[0]], w[1])
                ins = s.dma_start(out=c.wst[i % 3][:, :], in_=vsrc(cc, ov))
                c.wdm[i] = (f"ws{i % 3}", _mk(c, f"ws{i % 3}", ins, 16))

            @block.vector
            def _(v, cc=cc, i=i):
                v.wait_ge(c.S[c.wdm[i][0]], c.wdm[i][1])
                v.wait_ge(c.S["sp"], c.wpm.get(i - 4, c.wpm_e))
                c.wsm[i] = ("sv", _mk(c, "sv", v.tensor_copy(c.wbf[i % 4][:, :], c.wst[i % 3][:, :])))

            @block.tensor
            def _(te, ov=ov, cc=cc, i=i):
                te.wait_ge(c.S[c.wsm[i][0]], c.wsm[i][1])
                last = None
                for g in range(NG):
                    last = te.matmul(
                        c.ps[g][:, :],
                        c.arena[:, cc * TPC + g * 128: cc * TPC + (g + 1) * 128],
                        c.wbf[i % 4][:, :], start=(cc == 0), stop=(cc == CC - 1))
                c.wpm[i] = _mk(c, "sp", last)

        mm_done = c.N["sp"]
        for g in range(NG):
            @block.vector
            def _(v, g=g, mm_done=mm_done):
                v.wait_ge(c.S["sp"], mm_done)
                v.wait_ge(c.S["sw"], c.N["sw"])
                _mk(c, "sv", v.tensor_copy(c.bt[g % 2][:, :], c.ps[g][:, :]))

            @block.sync
            def _(s, ov=ov, g=g):
                s.wait_ge(c.S["sv"], c.N["sv"])
                _mk(c, "sw", s.dma_start(
                    out=c.vc[g * 128:(g + 1) * 128, ov * 512:(ov + 1) * 512],
                    in_=c.bt[g % 2][:, :]), 16)

        for cc in range(CC):
            i = ov * 64 + 32 + cc

            @block.sync
            def _(s, ov=ov, cc=cc, i=i):
                w = c.wsm.get(i - 3, c.wsm_e)
                s.wait_ge(c.S[w[0]], w[1])
                ins = s.dma_start(out=c.wst[i % 3][:, :], in_=vsrc(cc, ov))
                c.wdm[i] = (f"ws{i % 3}", _mk(c, f"ws{i % 3}", ins, 16))

            @block.vector
            def _(v, cc=cc, i=i):
                v.wait_ge(c.S[c.wdm[i][0]], c.wdm[i][1])
                v.wait_ge(c.S["sp"], c.wpm.get(i - 4, c.wpm_e))
                c.wsm[i] = ("sv", _mk(c, "sv", v.tensor_copy(c.wbf[i % 4][:, :], c.wst[i % 3][:, :])))

            @block.tensor
            def _(te, cc=cc, i=i):
                te.wait_ge(c.S[c.wsm[i][0]], c.wsm[i][1])
                ins = te.matmul(c.ps[0][0:10, :], c.awt_sb[:, cc * 10:(cc + 1) * 10],
                                c.wbf[i % 4][:, :], start=(cc == 0), stop=(cc == CC - 1))
                c.wpm[i] = _mk(c, "sp", ins)

        @block.vector
        def _(v, ov=ov):
            v.wait_ge(c.S["sp"], c.N["sp"])
            _mk(c, "sv", v.tensor_copy(c.av_sb[0:10, ov * 512:(ov + 1) * 512],
                                       c.ps[0][0:10, :]))


def _p4_gather(c):
    block = c.block

    @block.gpsimd
    def _(g):
        g.wait_ge(c.S["sw"], c.N["sw"])
        _mkcc(c, "cka", g.collective_compute(
            "AllGather", ALU.bypass, replica_groups=[list(range(NCORES))],
            ins=[c.kc.opt()], outs=[c.kg.opt()]))
        _mkcc(c, "cva", g.collective_compute(
            "AllGather", ALU.bypass, replica_groups=[list(range(NCORES))],
            ins=[c.vc.opt()], outs=[c.vg.opt()]))


def _p5_attn(c):
    block = c.block
    k_sbs, v_sbs, pt_ring, q_sbs, rbc = _p5_views(c)

    first = True
    prev_evac_sv = c.N["sv"]
    den_mark = {}
    prev_pa = c.qn_ready
    c.bcast_done = 0
    for b in range(B):
        for h in range(32):
            par = (b * 32 + h) % 2
            k_sb = k_sbs[par]
            v_sb = v_sbs[par]
            q_sb = q_sbs[par]
            kv_free = c.kvf[par]
            q_free = c.qf[par]
            ring_guard = [den_mark.get(32 + j, 0) for j in range(4)]

            @block.sync
            def _(s, b=b, h=h, first=first, kv_free=kv_free, q_free=q_free,
                  k_sb=k_sb, q_sb=q_sb, par=par):
                if first:
                    s.wait_ge(c.S["cka"], 1)
                if par == 1:
                    s.wait_ge(c.S["sg"], c.ln_sg)
                s.wait_ge(c.S["sp"], kv_free)
                kv = f"kv{par}"
                if b == 0:
                    kgv = c.kg.rearrange("(r q) t -> q r t", q=4128)
                    _mk(c, kv, s.dma_start(
                        out=k_sb[:, 0:4096],
                        in_=kgv[h * 128:(h + 1) * 128, :, 0:512]), 16)
                else:
                    kap = bass_rust.AP(c.kg.tensor, 7 * 4128 * 1024 + h * 128 * 1024 + 512,
                                       [[1024, 128], [-4128 * 1024, 8], [1, 512]])
                    _mk(c, kv, s.dma_start(out=k_sb[:, 0:4096], in_=kap), 16)
                _mk(c, kv, s.dma_start(
                    out=k_sb[:, 8 * 512:9 * 512],
                    in_=c.kc[h * 128:(h + 1) * 128, b * 512:(b + 1) * 512]), 16)
                s.wait_ge(c.S["sp"], q_free)
                _mk(c, kv, s.dma_start(
                    out=q_sb[:, :],
                    in_=c.qst[h * 128:(h + 1) * 128, b * 512:(b + 1) * 512]), 16)
            kvsem = f"kv{par}"

            @block.gpsimd
            def _(g, b=b, h=h, first=first, kv_free=kv_free, v_sb=v_sb, par=par):
                if first:
                    g.wait_ge(c.S["cva"], 1)
                    g.wait_ge(c.S["cka"], 1)
                if par == 1:
                    g.wait_ge(c.S["sg"], c.ln_sg)
                g.wait_ge(c.S["sp"], kv_free)
                kv = f"kv{par}"
                for kt in range(8):
                    r = kt if b == 0 else 7 - kt
                    vsrc = c.vg[r * TPC + b * 512: r * TPC + b * 512 + 512,
                                h * 128:(h + 1) * 128].rearrange("(j p) e -> p j e", p=128)
                    _mk(c, kv, g.dma_start(
                        out=v_sb[:, kt * 512:(kt + 1) * 512], in_=vsrc), 16)
                _mk(c, kv, g.dma_start(
                    out=v_sb[:, 4096:4608],
                    in_=c.vc[b * 512:(b + 1) * 512,
                             h * 128:(h + 1) * 128].rearrange("(j p) e -> p j e", p=128)), 16)
                _mk(c, kv, g.dma_start(
                    out=c.tinybf[0:1, 0:16],
                    in_=c.kg.rearrange("(r q) t -> q r t", q=4128)[4096 + h][:, 0:2]), 16)
            kvload = c.N[kvsem]

            @block.vector
            def _(v, b=b, h=h, kvsem=kvsem, kvload=kvload, bc=c.bcast_done):
                v.wait_ge(c.S[kvsem], kvload)
                v.wait_ge(c.S["sp"], bc)
                _ch(c, v, "sv", v.reduce_max(c.tiny[0:1, 0:1], c.tinybf[0:1, 0:16], axis=AX.X))
                _ch(c, v, "sv", v.tensor_mul(c.tiny[0:1, 3:4], c.tiny[0:1, 0:1],
                                             c.qmaxrow[0:1, h * 2 + b: h * 2 + b + 1]))
                _mk(c, "sv", v.tensor_mul(c.tiny[0:1, 4:5],
                                          c.qmaxrow[0:1, h * 2 + b: h * 2 + b + 1],
                                          c.akmaxrow[0:1, h:h + 1]))

            @block.scalar
            def _(sc):
                sc.wait_ge(c.S["sv"], c.N["sv"])
                _ch(c, sc, "sa", sc.activation(c.tiny[0:1, 5:7], c.tiny[0:1, 3:5], AF.Ln,
                                               bias=f32(0.0), scale=f32(1.0)))
                _mk(c, "sa", sc.activation(c.tiny[0:1, 7:9], c.tiny[0:1, 5:7], AF.Exp,
                                           bias=f32(0.0), scale=f32(0.5)))

            @block.vector
            def _(v):
                v.wait_ge(c.S["sa"], c.N["sa"])
                _ch(c, v, "sv", v.tensor_scalar(c.tiny[0:1, 9:10], c.tiny[0:1, 7:8],
                                                -RSQD, None, ALU.mult))
                _ch(c, v, "sv", v.tensor_scalar(c.tiny[0:1, 10:11], c.tiny[0:1, 8:9],
                                                -1.0 / 64.0, None, ALU.mult))
                _mk(c, "sv", v.tensor_copy(c.tinybf[0:1, 16:18], c.tiny[0:1, 9:11]))

            @block.tensor
            def _(te):
                te.wait_ge(c.S["sv"], c.N["sv"])
                te.wait_ge(c.S["sa"], c.N["sa"])
                _mk(c, "sp", te.matmul(c.ps[7][:, 256:258], c.ones_row[0:1, :],
                                       c.tinybf[0:1, 16:18], start=True, stop=True))
            c.bcast_done = c.N["sp"]

            @block.vector
            def _(v, b=b):
                v.wait_ge(c.S["sp"], c.N["sp"])
                v.wait_ge(c.S["sa"], c.N["sa"])
                _ch(c, v, "sv", v.tensor_copy(c.nmk[:, 0:2], c.ps[7][:, 256:258]))
                _ch(c, v, "sv", v.tensor_scalar(c.btab[:, 0:8], c.dead_sb[:, b * 8:(b + 1) * 8],
                                                c.nmk[:, 0:1], None, ALU.add))
                _ch(c, v, "sv", v.tensor_copy(c.btab[:, 8:9], c.nmk[:, 0:1]))
                _mk(c, "sv", v.tensor_copy(c.btab[:, 9:10], c.nmk[:, 1:2]))
            bias_ready = c.N["sv"]

            score_mark = {}
            asc_mark = {}
            sa0 = c.N["sa"]
            sg0 = c.N["sg"]

            @block.tensor
            def _(te, kvsem=kvsem, kvload=kvload, prev_evac_sv=prev_evac_sv, sa0=sa0,
                  sg0=sg0, k_sb=k_sb, v_sb=v_sb, q_sb=q_sb, h=h,
                  bias_ready=bias_ready, prev_pa=prev_pa):
                sbank = [0, 1, 5, 6]
                for step in range(40):
                    ss = step
                    if ss < 36:
                        if ss == 0:
                            te.wait_ge(c.S[kvsem], kvload)
                        if ss < 4:
                            te.wait_ge(c.S["sa"], max(0, sa0 - 6))
                        else:
                            te.wait_ge(c.S["sa"], sa0 + ss - 3)
                        ins = te.matmul(c.ps[sbank[ss % 4]][:, :],
                                        k_sb[:, ss * 128:(ss + 1) * 128],
                                        q_sb[:, :], start=True, stop=True)
                        score_mark[ss] = _mk(c, "sp", ins)
                    if step >= 4:
                        sy = step - 4
                        te.wait_ge(c.S["sa"], sa0 + sy + 1)
                        if sy >= 32:
                            te.wait_ge(c.S["sg"], sg0 + (sy - 32) + 1)
                        if sy == 0:
                            te.wait_ge(c.S["sv"], prev_evac_sv)
                        te.matmul(c.ps[2][:, :], v_sb[:, sy * 128:(sy + 1) * 128],
                                  pt_ring[:, (sy % 4) * 512:(sy % 4 + 1) * 512],
                                  start=(sy == 0), stop=(sy == 35))
                        ins = te.matmul(c.ps[4][0:1, :], c.ones_bf[:, :],
                                        pt_ring[:, (sy % 4) * 512:(sy % 4 + 1) * 512],
                                        start=(sy == 0), stop=(sy == 35))
                        den_mark[sy] = _mk(c, "sp", ins)
                te.wait_ge(c.S["sv"], max(prev_evac_sv, bias_ready))
                te.wait_ge(c.S["sa"], prev_pa)
                for qt in range(4):
                    ins = te.matmul(c.ps[7][:, qt * 16: qt * 16 + 10],
                                    q_sb[:, qt * 128:(qt + 1) * 128],
                                    c.ak_sb[:, h * 10:(h + 1) * 10], start=True, stop=True)
                    asc_mark[qt] = _mk(c, "sp", ins)
            c.qf[par] = c.N["sp"]
            c.kvf[par] = den_mark[35]

            @block.scalar
            def _(sc, bias_ready=bias_ready, ring_guard=ring_guard):
                sc.wait_ge(c.S["sv"], bias_ready)
                for ss in range(36):
                    slot = ss // 4
                    col = slot if slot < 8 else 8
                    sc.wait_ge(c.S["sp"], score_mark[ss])
                    if ss >= 4:
                        sc.wait_ge(c.S["sp"], den_mark[ss - 4])
                    elif ring_guard[ss] > 0:
                        sc.wait_ge(c.S["sp"], ring_guard[ss])
                    _mk(c, "sa", sc.activation(
                        pt_ring[:, (ss % 4) * 512:(ss % 4 + 1) * 512],
                        c.ps[[0, 1, 5, 6][ss % 4]][:, :], AF.Exp,
                        bias=c.btab[:, col:col + 1], scale=f32(RSQD)))
                for qt in range(4):
                    sc.wait_ge(c.S["sp"], asc_mark[qt])
                    _mk(c, "sa", sc.activation(
                        c.pa_sb[:, qt * 10:(qt + 1) * 10], c.ps[7][:, qt * 16: qt * 16 + 10],
                        AF.Exp, bias=c.btab[:, 9:10], scale=f32(1.0 / 64.0),
                        accum_out=c.small[:, 40 + qt:41 + qt]))
            pa_done = c.N["sa"]
            prev_pa = pa_done

            @block.gpsimd
            def _(g, sa0=sa0):
                for ss in range(32, 36):
                    g.wait_ge(c.S["sa"], sa0 + ss + 1)
                    _mk(c, "sg", g.tensor_mul(
                        pt_ring[:, (ss % 4) * 512:(ss % 4 + 1) * 512],
                        pt_ring[:, (ss % 4) * 512:(ss % 4 + 1) * 512],
                        c.band_sb[:, (ss - 32) * 512:(ss - 31) * 512]))

            @block.vector
            def _(v, pa_done=pa_done):
                v.wait_ge(c.S["sa"], pa_done)
                _ch(c, v, "sv", v.reciprocal(c.small[:, 44:48], c.small[:, 40:44]))
                _mk(c, "sv", v.tensor_scalar(c.small[:, 48:52], c.small[:, 44:48],
                                             c.gat_sb[:, 0:1], None, ALU.mult))
            f2_mark = c.N["sv"]

            @block.gpsimd
            def _(g, f2_mark=f2_mark):
                g.wait_ge(c.S["sv"], f2_mark)
                for qt in range(4):
                    _mk(c, "sg", g.tensor_scalar(c.pa_sb[:, qt * 10:(qt + 1) * 10],
                                                 c.pa_sb[:, qt * 10:(qt + 1) * 10],
                                                 c.small[:, 48 + qt:49 + qt], None, ALU.mult))
            scale_done = c.N["sg"]

            @block.tensor
            def _(te, scale_done=scale_done, sa0=sa0):
                te.wait_ge(c.S["sg"], scale_done)
                te.wait_ge(c.S["sa"], sa0 + 36)
                te.wait_ge(c.S["sv"], c.N["sv"])
                for qt in range(4):
                    _mk(c, "sp", te.transpose(
                        c.ps[5][:, qt * 64:(qt + 1) * 64].bitcast(BF16)[0:10, :],
                        c.pa_sb[:, qt * 10:(qt + 1) * 10], c.idn[:, :]))
            paT_done = c.N["sp"]

            @block.vector
            def _(v, paT_done=paT_done):
                v.wait_ge(c.S["sp"], paT_done)
                _mk(c, "sv", v.tensor_copy(c.paT_sb[0:10, :],
                                           c.ps[5][:, 0:256].bitcast(BF16)[0:10, :]))
            paTc = c.N["sv"]

            @block.tensor
            def _(te, h=h, paTc=paTc):
                te.wait_ge(c.S["sv"], paTc)
                for qt in range(4):
                    _mk(c, "sp", te.matmul(c.ps[3][:, qt * 128:(qt + 1) * 128],
                                           c.av_sb[0:10, h * 128:(h + 1) * 128],
                                           c.paT_sb[0:10, qt * 128:(qt + 1) * 128],
                                           start=True, stop=True))
            yTa_done = c.N["sp"]

            @block.vector
            def _(v, yTa_done=yTa_done):
                v.wait_ge(c.S["sp"], yTa_done)
                _mk(c, "sv", v.reciprocal(c.rrow[0:1, :], c.ps[4][0:1, 0:512]))

            @block.tensor
            def _(te, pa_done=pa_done):
                te.wait_ge(c.S["sv"], c.N["sv"])
                te.wait_ge(c.S["sa"], pa_done)
                _mk(c, "sp", te.matmul(c.ps[7][:, 0:512], c.ones_row_f[0:1, :],
                                       c.rrow[0:1, :], start=True, stop=True))

            @block.vector
            def _(v, b=b, h=h):
                v.wait_ge(c.S["sp"], c.N["sp"])
                _ch(c, v, "sv", v.tensor_copy(rbc[:, :], c.ps[7][:, 0:512]))
                _ch(c, v, "sv", v.tensor_mul(c.ev[0][:, :], c.ps[2][:, :], rbc[:, :]))
                _mk(c, "sv", v.tensor_add(
                    c.arena[:, h * TPC + b * 512: h * TPC + (b + 1) * 512],
                    c.ev[0][:, :], c.ps[3][:, :]))
            prev_evac_sv = c.N["sv"]

            first = False
    c.xt_free_sp = c.N["sp"]


def _p5_proj(c):
    block = c.block

    c.wsm, c.wdm, c.wpm = {}, {}, {}
    c.wsm_e = ("sg", c.N["sg"])
    c.wpm_e = c.N["sp"]
    for po in range(8):
        for cc in range(CC):
            i = po * 32 + cc

            @block.sync
            def _(s, po=po, cc=cc, i=i):
                w = c.wsm.get(i - 3, c.wsm_e)
                s.wait_ge(c.S[w[0]], w[1])
                src = c.wprojT.rearrange("(cc p) o -> cc p o", p=128)[cc][
                    :, po * 512:(po + 1) * 512]
                ins = s.dma_start(out=c.wst[i % 3][:, :], in_=src)
                c.wdm[i] = (f"ws{i % 3}", _mk(c, f"ws{i % 3}", ins, 16))

            @block.vector
            def _(v, cc=cc, i=i):
                v.wait_ge(c.S[c.wdm[i][0]], c.wdm[i][1])
                v.wait_ge(c.S["sp"], c.wpm.get(i - 4, c.wpm_e))
                c.wsm[i] = ("sv", _mk(c, "sv", v.tensor_copy(c.wbf[i % 4][:, :], c.wst[i % 3][:, :])))

            @block.tensor
            def _(te, po=po, cc=cc, i=i):
                te.wait_ge(c.S[c.wsm[i][0]], c.wsm[i][1])
                if cc == 0:
                    if po == 0:
                        te.wait_ge(c.S["sa"], c.N["sa"])
                    for g in range(8):
                        te.wait_ge(c.S["sv"], c.evm.get(g, 0))
                last = None
                for g in range(8):
                    last = te.matmul(
                        c.ps[g][:, :],
                        c.arena[:, cc * TPC + g * 128: cc * TPC + (g + 1) * 128],
                        c.wbf[i % 4][:, :], start=(cc == 0), stop=(cc == CC - 1))
                c.wpm[i] = _mk(c, "sp", last)

        _evac_residual(c, po, c.x_own, c.x1, scale_bias=False, final_out=False)


def _evac_residual(c, og, xin, xout, scale_bias, final_out):
    block = c.block
    mm_done = c.N["sp"]

    fbl = 0
    if scale_bias:
        @block.scalar
        def _(s, og=og):
            s.wait_ge(c.S["sv"], c.N["sv"])
            _mk(c, f"fb{og % 2}", s.dma_start(
                out=c.fb_sb[og % 2][:, :],
                in_=c.fbias_d[0:1, og * 512:(og + 1) * 512].partition_broadcast(128).squeeze(1)), 16)
        fbl = c.N[f"fb{og % 2}"]

    for g in range(8):
        @block.scalar
        def _(s, og=og, g=g):
            s.wait_ge(c.S["pf"], max(0, c.N["pf"] - 16))
            _mk(c, f"xs{g % 2}", s.dma_start(
                out=c.xs[g % 2][:, :],
                in_=xin[g * 128:(g + 1) * 128, og * 512:(og + 1) * 512]), 16)
        xsl = c.N[f"xs{g % 2}"]

        @block.vector
        def _(v, og=og, g=g, mm_done=mm_done, xsl=xsl, fbl=fbl):
            v.wait_ge(c.S["sp"], mm_done)
            v.wait_ge(c.S[f"xs{g % 2}"], xsl)
            if scale_bias:
                v.wait_ge(c.S[f"fb{og % 2}"], fbl)
            v.wait_ge(c.S["pf"], max(0, c.N["pf"] - 16))
            if scale_bias:
                _ch(c, v, "sv", v.scalar_tensor_tensor(c.ev[2 + g % 2][:, :], c.ps[g][:, :],
                                       c.fsc_sb[:, 0:1], c.fb_sb[og % 2][:, :],
                                       ALU.mult, ALU.add))
                c.evm[g] = _mk(c, "sv", v.tensor_add(c.ev[g % 2][:, :], c.ev[2 + g % 2][:, :],
                                          c.xs[g % 2][:, :]))
            else:
                c.evm[g] = _mk(c, "sv", v.tensor_add(c.ev[g % 2][:, :], c.ps[g][:, :],
                                                     c.xs[g % 2][:, :]))

        @block.scalar
        def _(s, og=og, g=g):
            s.wait_ge(c.S["sv"], c.N["sv"])
            _mk(c, "pf", s.dma_start(
                out=xout[g * 128:(g + 1) * 128, og * 512:(og + 1) * 512],
                in_=c.ev[g % 2][:, :]), 16)


def _p6_ffn(c, xin, xout):
    block = c.block
    final_out = xout is c.out

    c.wdm, c.wpm = {}, {}
    c.wpm_e = c.N["sp"]
    for og in range(8):
        for cc in range(CC):
            i = og * 32 + cc

            @block.sync
            def _(s, og=og, cc=cc, i=i):
                if og == 0 and cc == 0:
                    s.wait_ge(c.S["cfw"], 1)
                s.wait_ge(c.S["sp"], c.wpm.get(i - 4, c.wpm_e))
                src = c.wfg[og * D + cc * 128: og * D + (cc + 1) * 128, 0:512]
                ins = s.dma_start(out=c.wbf[i % 4][:, :], in_=src)
                c.wdm[i] = (f"wb{i % 4}", _mk(c, f"wb{i % 4}", ins, 16))

            @block.tensor
            def _(te, og=og, cc=cc, i=i):
                te.wait_ge(c.S[c.wdm[i][0]], c.wdm[i][1])
                if og == 0 and cc == 0:
                    te.wait_ge(c.S["sv"], c.N["sv"])
                if cc == 0:
                    for g in range(8):
                        te.wait_ge(c.S["sv"], c.evm.get(g, 0))
                last = None
                for g in range(8):
                    last = te.matmul(
                        c.ps[g][:, :],
                        c.arena[:, cc * TPC + g * 128: cc * TPC + (g + 1) * 128],
                        c.wbf[i % 4][:, :], start=(cc == 0), stop=(cc == CC - 1))
                c.wpm[i] = _mk(c, "sp", last)

        _evac_residual(c, og, xin, xout, scale_bias=True, final_out=final_out)


def _p7_attnb(c):
    block = c.block
    sig, retr = _p7_views(c)

    c.wsm, c.wdm, c.wpm = {}, {}, {}
    c.wsm_e = ("sg", c.N["sg"])
    c.wpm_e = c.N["sp"]
    for h8 in range(8):
        for cq in range(8):
            i = h8 * 8 + cq

            @block.sync
            def _(s, h8=h8, cq=cq, i=i):
                w = c.wsm.get(i - 3, c.wsm_e)
                s.wait_ge(c.S[w[0]], w[1])
                src = c.wqselT.rearrange("(cq ch p) (h j) -> cq h p ch j",
                                         ch=4, p=128, j=128)[cq, h8]
                ins = s.dma_start(out=c.wst[i % 3][:, :], in_=src)
                c.wdm[i] = (f"ws{i % 3}", _mk(c, f"ws{i % 3}", ins, 16))

            @block.vector
            def _(v, cq=cq, i=i):
                v.wait_ge(c.S[c.wdm[i][0]], c.wdm[i][1])
                v.wait_ge(c.S["sp"], c.wpm.get(i - 4, c.wpm_e))
                c.wsm[i] = ("sv", _mk(c, "sv", v.tensor_copy(c.wbf[i % 4][:, :], c.wst[i % 3][:, :])))

            @block.tensor
            def _(te, cq=cq, i=i):
                te.wait_ge(c.S[c.wsm[i][0]], c.wsm[i][1])
                last = None
                for ch in range(4):
                    cc = cq * 4 + ch
                    for th in range(2):
                        last = te.matmul(
                            c.ps[th][:, :], c.wbf[i % 4][:, ch * 128:(ch + 1) * 128],
                            c.arena[:, cc * TPC + th * 512: cc * TPC + th * 512 + 512],
                            start=(cc == 0), stop=(cc == CC - 1))
                c.wpm[i] = _mk(c, "sp", last)

        mm_done = c.N["sp"]
        for th in range(2):
            @block.vector
            def _(v, h8=h8, th=th, mm_done=mm_done):
                v.wait_ge(c.S["sp"], mm_done)
                _ch(c, v, "sv", v.tensor_scalar(c.ev[0][:, :], c.ps[th][:, :], c.bq_sb[:, h8:h8 + 1],
                                None, ALU.add))
                _mk(c, "sv", v.tensor_scalar_min(c.ev[1][:, :], c.ev[0][:, :], 0.0))

            @block.scalar
            def _(sc):
                sc.wait_ge(c.S["sv"], c.N["sv"])
                _mk(c, "sa", sc.activation(c.ev[2][:, :], c.ev[1][:, :], AF.Exp,
                                           bias=f32(0.0), scale=f32(1.0)))

            @block.vector
            def _(v, h8=h8, th=th):
                v.wait_ge(c.S["sa"], c.N["sa"])
                _ch(c, v, "sv", v.tensor_relu(c.ev[3][:, :], c.ev[0][:, :]))
                _mk(c, "sv", v.tensor_add(
                    sig[:, h8 * TPC + th * 512: h8 * TPC + (th + 1) * 512],
                    c.ev[2][:, :], c.ev[3][:, :]))

    for h8 in range(8):
        for tt in range(8):
            @block.tensor
            def _(te, h8=h8, tt=tt):
                te.wait_ge(c.S["sv"], c.N["sv"])
                _mk(c, "sp", te.matmul(c.ps[4][:, 0:129],
                                       sig[:, h8 * TPC + tt * 128: h8 * TPC + (tt + 1) * 128],
                                       c.Mz_sb[:, h8 * 129:(h8 + 1) * 129],
                                       start=True, stop=True))

            @block.vector
            def _(v, tt=tt):
                v.wait_ge(c.S["sp"], c.N["sp"])
                _ch(c, v, "sv", v.tensor_scalar(c.small[:, 40:41], c.ps[4][:, 128:129], 1e-6, None, ALU.add))
                _ch(c, v, "sv", v.reciprocal(c.small[:, 41:42], c.small[:, 40:41]))
                _mk(c, "sv", v.tensor_scalar(c.bt[0][:, 0:128], c.ps[4][:, 0:128],
                                             c.small[:, 41:42], None, ALU.mult))

            _transpose_batch(c, [
                (c.bt[0][:, 0:128],
                 retr[:, h8 * TPC + tt * 128: h8 * TPC + (tt + 1) * 128])
            ])

    c.wdm, c.wpm = {}, {}
    c.wpm_e = c.N["sp"]
    for og in range(8):
        for c8 in range(8):
            i = og * 8 + c8

            @block.sync
            def _(s, og=og, c8=c8, i=i):
                if og == 0 and c8 == 0:
                    s.wait_ge(c.S["cwo"], 1)
                s.wait_ge(c.S["sp"], c.wpm.get(i - 4, c.wpm_e))
                src = c.wog[c8 * 128:(c8 + 1) * 128, og * 512:(og + 1) * 512]
                ins = s.dma_start(out=c.wbf[i % 4][:, :], in_=src)
                c.wdm[i] = (f"wb{i % 4}", _mk(c, f"wb{i % 4}", ins, 16))

            @block.tensor
            def _(te, og=og, c8=c8, i=i):
                te.wait_ge(c.S[c.wdm[i][0]], c.wdm[i][1])
                if c8 == 0:
                    if og == 0:
                        te.wait_ge(c.S["sv"], c.N["sv"])
                    for g in range(8):
                        te.wait_ge(c.S["sv"], c.evm.get(g, 0))
                last = None
                for g in range(8):
                    last = te.matmul(
                        c.ps[g][:, :],
                        retr[:, c8 * TPC + g * 128: c8 * TPC + (g + 1) * 128],
                        c.wbf[i % 4][:, :], start=(c8 == 0), stop=(c8 == 7))
                c.wpm[i] = _mk(c, "sp", last)

        _evac_residual(c, og, c.x2, c.x3, scale_bias=False, final_out=False)


def _final(c):
    block = c.block

    @block.sync
    def _(s):
        s.wait_ge(c.S["sw"], c.N["sw"])
        s.wait_ge(c.S["pf"], c.N["pf"])
        for nm, (tap, src) in c.taps.items():
            _mk(c, "pf", s.dma_start(out=tap, in_=src), 16)
        s.wait_ge(c.S["pf"], c.N["pf"])
        s.wait_ge(c.S["sx"], c.N["sx"])
        s.wait_ge(c.S["sg"], c.N["sg"])
        s.wait_ge(c.S["sa"], c.N["sa"])
        s.wait_ge(c.S["sv"], c.N["sv"])
        s.wait_ge(c.S["sp"], c.N["sp"])



def kernel(**inputs):
    in_maps = _host_prep(inputs)
    nc = build_program()
    res = run_bass_kernel_spmd(nc, in_maps, core_ids=list(range(NCORES)))
    out = np.empty((B, T, D), np.float32)
    for ci in range(NCORES):
        o = res.results[ci]["out"]
        out[0, ci * CHUNK:(ci + 1) * CHUNK] = o[:CHUNK]
        out[1, (7 - ci) * CHUNK:(8 - ci) * CHUNK] = o[CHUNK:]
    return out
